# version 05
# baseline (speedup 1.0000x reference)
import numpy as np
from contextlib import ExitStack

import concourse.bass as bass
import concourse.mybir as mybir
from concourse.bass import IndirectOffsetOnAxis
from concourse.bass_utils import run_bass_kernel_spmd
from concourse.replica_groups import maybe_share_collective_output_space

F32 = mybir.dt.float32
BF16 = mybir.dt.bfloat16
I32 = mybir.dt.int32
AF = mybir.ActivationFunctionType
ALU = mybir.AluOpType
AX = mybir.AxisListType
ET = mybir.EngineType

RG = [[0, 1, 2, 3, 4, 5, 6, 7]]
NCORE = 8
T = 2048
NT = T // 128
NTOK = NCORE * T
NCH = NTOK // 128
D = 1024
ND = D // 128
FF = 4096
NF = FF // 128
E = 8
CAP = 2560
SB = 256
NSB = CAP // SB
H = 16
TC = 256
NTC = T // TC


class Stream:
    def __init__(self, ctx, nc, name):
        self.h = ctx.enter_context(nc.semaphore(name))
        self.n = 0

    def inc(self, instr, k=1):
        instr.then_inc(self.h, k)
        self.n += k
        return instr

    def wait(self, eng, n=None):
        eng.wait_ge(self.h, self.n if n is None else n)


class Arena:
    def __init__(self, tensor, n, name):
        self.t, self.n, self.off, self.marks, self.name = tensor, n, 0, [], name
        self.peak = 0

    def take(self, width):
        assert self.off + width <= self.n, \
            f"{self.name} overflow {self.off}+{width}>{self.n}"
        ap = self.t[:, self.off:self.off + width]
        self.off += width
        self.peak = max(self.peak, self.off)
        return ap

    def push(self):
        self.marks.append(self.off)

    def pop(self):
        self.off = self.marks.pop()


_PE_RUN = [0]


class TEng:

    def __init__(self, t):
        self._t = t

    def matmul(self, out, lhsT, rhs, start=None, stop=None, **k):
        s = True if start is None else start
        p = True if stop is None else stop
        r = self._t.matmul(out, lhsT, rhs, start=s, stop=p, **k)
        if not (s and p):
            try:
                w = 1
                for st_, nu in list(rhs.ap)[1:]:
                    w += abs(st_) * (nu - 1)
            except Exception:
                w = 512
            if w > 256:
                n = 5
            elif w > 128:
                n = 3
            elif w > 64:
                n = 2
            else:
                n = 1
            for _ in range(n):
                self._t.nop()
        return r

    def nop(self, *a, **k):
        return self._t.nop(*a, **k)

    def __getattr__(self, name):
        return getattr(self._t, name)


_V_PEND = []
_V_WIN = 4


def _v_span(ap):
    t = getattr(ap, "tensor", None)
    if t is None:
        return None
    key = getattr(t, "name", None) or id(t)
    try:
        off = ap.offset
        dims = list(ap.ap)
        w = 1
        for step, num in dims[1:]:
            w += abs(step) * (num - 1)
        return (key, off, off + w)
    except Exception:
        return (key, -1 << 40, 1 << 40)


class VEng:

    _RED = {"reduce_sum", "reduce_max", "reduce_min", "max",
            "tensor_tensor_scan", "reciprocal"}

    def __init__(self, v):
        self._v = v

    def _raw_nop(self):
        self._v.engine_nop()
        self._age()

    def _age(self):
        keep = []
        for p in _V_PEND:
            p[3] += 1
            if p[3] < _V_WIN:
                keep.append(p)
        _V_PEND[:] = keep

    def _pad_for(self, spans):
        need = 0
        for s in spans:
            if s is None:
                continue
            for p in _V_PEND:
                if p[0] == s[0] and s[1] < p[2] and p[1] < s[2]:
                    need = max(need, _V_WIN - p[3])
        for _ in range(need):
            self._raw_nop()

    def engine_nop(self, *a, **k):
        while _V_PEND:
            self._raw_nop()
        r = self._v.engine_nop(*a, **k)
        self._age()
        return r

    def wait_ge(self, *a, **k):
        r = self._v.wait_ge(*a, **k)
        self._age()
        return r

    def __getattr__(self, name):
        f = getattr(self._v, name)
        if not callable(f):
            return f
        red = name in VEng._RED

        def g(*a, **k):
            wr = k.get("out", None)
            spans = []
            for i, x in enumerate(a):
                if getattr(x, "tensor", None) is None:
                    continue
                if wr is None and not spans and "out" not in k:
                    wr = x
                    continue
                spans.append(_v_span(x))
            for kk, x in k.items():
                if kk != "out" and getattr(x, "tensor", None) is not None:
                    spans.append(_v_span(x))
            wsp = _v_span(wr) if wr is not None else None
            self._pad_for(spans + ([wsp] if wsp is not None else []))
            r = f(*a, **k)
            self._age()
            if wsp is not None:
                _V_PEND.append([wsp[0], wsp[1], wsp[2], 0])
            if red:
                for _ in range(_V_WIN):
                    self._raw_nop()
            return r
        return g


def bcast8(ap, reps):
    return ap.unsqueeze(2).broadcast_to((ap.shape[0], ap.shape[1], reps))


def grp(ap, e=8):
    return ap.rearrange("p (c e) -> p c e", e=e)


def build_nc(stage=4):
    _PE_RUN[0] = 0
    nc = bass.Bass("TRN2", target_bir_lowering=False,
                   detect_race_conditions=False)

    x_in = nc.dram_tensor("x", [T, D], F32, kind="ExternalInput").ap()
    Wk_in = nc.dram_tensor("Wk", [D, D], F32, kind="ExternalInput").ap()
    Wv_in = nc.dram_tensor("Wv", [D, D], F32, kind="ExternalInput").ap()
    Wr_in = nc.dram_tensor("Wr", [D, D], F32, kind="ExternalInput").ap()
    Wo_in = nc.dram_tensor("Wo", [D, D], F32, kind="ExternalInput").ap()
    mixk = nc.dram_tensor("mixk", [D], F32, kind="ExternalInput").ap()
    mixv = nc.dram_tensor("mixv", [D], F32, kind="ExternalInput").ap()
    mixr = nc.dram_tensor("mixr", [D], F32, kind="ExternalInput").ap()
    td_in = nc.dram_tensor("td", [H], F32, kind="ExternalInput").ap()
    tf_in = nc.dram_tensor("tf", [H], F32, kind="ExternalInput").ap()
    wr_in = nc.dram_tensor("wrout", [D, E], F32, kind="ExternalInput").ap()
    br_in = nc.dram_tensor("brout", [E], F32, kind="ExternalInput").ap()
    w1_in = nc.dram_tensor("w1e", [D, FF], F32, kind="ExternalInput").ap()
    b1_in = nc.dram_tensor("b1e", [FF], F32, kind="ExternalInput").ap()
    w2_in = nc.dram_tensor("w2e", [FF, D], F32, kind="ExternalInput").ap()
    b2_in = nc.dram_tensor("b2e", [D], F32, kind="ExternalInput").ap()
    es_in = nc.dram_tensor("esel", [E], F32, kind="ExternalInput").ap()
    y_out = nc.dram_tensor("y", [T, D], F32, kind="ExternalOutput").ap()

    dbg = {}
    ag_marks = {}
    if stage in (1, 101, 102, 104, 105):
        dbg["dbg_pw"] = nc.dram_tensor("dbg_pw", [128, 1024], F32,
                                       kind="ExternalOutput").ap()
    if stage == 1:
        dbg["dbg_slot"] = nc.dram_tensor("dbg_slot", [128, 32], I32,
                                         kind="ExternalOutput").ap()
    if stage in (2, 21, 22, 23):
        dbg["dbg_buf"] = nc.dram_tensor("dbg_buf", [CAP, D], BF16,
                                        kind="ExternalOutput").ap()
    if stage in (2, 22, 23):
        dbg["dbg_oute"] = nc.dram_tensor("dbg_oute", [CAP, D], BF16,
                                         kind="ExternalOutput").ap()

    shr = maybe_share_collective_output_space("AllGather", RG)
    lnx_d = nc.dram_tensor("lnx_d", [T, D], BF16, kind="Internal").ap()
    LNX = nc.dram_tensor("LNX", [NTOK, D], BF16, kind="Internal",
                         addr_space=shr).ap()
    lg_d = nc.dram_tensor("lg_d", [128, 128], F32, kind="Internal").ap()
    LG = nc.dram_tensor("LG", [1024, 128], F32, kind="Internal",
                        addr_space=shr).ap()
    stag_d = nc.dram_tensor("stag_d", [NCH, 128], F32, kind="Internal").ap()
    buf_d = nc.dram_tensor("buf_d", [CAP, D], BF16, kind="Internal").ap()
    oute_d = nc.dram_tensor("oute_d", [CAP, D], BF16, kind="Internal").ap()
    OUTA = nc.dram_tensor("OUTA", [NCORE * CAP, D], BF16, kind="Internal",
                          addr_space=shr).ap()
    w1b_d = nc.dram_tensor("w1b_d", [D, FF], BF16, kind="Internal").ap()
    w2b_d = nc.dram_tensor("w2b_d", [FF, D], BF16, kind="Internal").ap()
    wtb_d = [nc.dram_tensor(f"wtb{i}", [D, D], BF16, kind="Internal").ap()
             for i in range(4)]
    ewu_d = nc.dram_tensor("ewu_d", [2, H], F32, kind="Internal").ap()
    pref_d = nc.dram_tensor("pref_d", [1024], F32, kind="Internal").ap()
    ogia_d = nc.dram_tensor("ogia_d", [128, 256], I32, kind="Internal").ap()
    oww_d = nc.dram_tensor("oww_d", [128, 256], F32, kind="Internal").ap()
    x2_d = nc.dram_tensor("x2_d", [T, D], F32, kind="Internal").ap()

    ctx = ExitStack()

    RESR = ctx.enter_context(nc.sbuf_tensor("resr", [128, NT * D], F32))
    RES = [RESR[:, D * k:D * (k + 1)] for k in range(NT)]
    W1V = [RESR[:, 2048 * d:2048 * (d + 1)].bitcast(BF16) for d in range(ND)]

    TRI = ctx.enter_context(nc.sbuf_tensor("tri", [128, 128], BF16))
    IOTJ = ctx.enter_context(nc.sbuf_tensor("iotj", [128, 128], F32))
    IOTOK = ctx.enter_context(nc.sbuf_tensor("iotok", [128, 128], F32))
    IOT8 = ctx.enter_context(nc.sbuf_tensor("iot8", [128, 8], F32))
    IOTK = ctx.enter_context(nc.sbuf_tensor("iotk", [128, 32], F32))
    ONES8 = ctx.enter_context(nc.sbuf_tensor("ones8", [8, 128], F32))
    ONESC = ctx.enter_context(nc.sbuf_tensor("onesc", [128, 1], BF16))
    ID8 = ctx.enter_context(nc.sbuf_tensor("id8", [8, 8], F32))
    ESELB = ctx.enter_context(nc.sbuf_tensor("eselb", [128, 8], F32))
    B2B = ctx.enter_context(nc.sbuf_tensor("b2b", [128, D], BF16))
    B1P = ctx.enter_context(nc.sbuf_tensor("b1p", [128, NF], F32))
    WRS = ctx.enter_context(nc.sbuf_tensor("wrs", [128, 64], BF16))
    BR8 = ctx.enter_context(nc.sbuf_tensor("br8", [8, 1], F32))
    BRB = ctx.enter_context(nc.sbuf_tensor("brb", [128, 8], F32))
    MIX = ctx.enter_context(nc.sbuf_tensor("mix", [128, 48], F32))
    EWU = ctx.enter_context(nc.sbuf_tensor("ewu", [16, 8], F32))
    EWP = ctx.enter_context(nc.sbuf_tensor("ewp", [128, 17], F32))
    HMT = ctx.enter_context(nc.sbuf_tensor("hmt", [16, 1024], BF16))
    HMM = ctx.enter_context(nc.sbuf_tensor("hmm", [128, 128], BF16))
    CST = ctx.enter_context(nc.sbuf_tensor("cst", [128, 8], F32))
    IDXT = ctx.enter_context(nc.sbuf_tensor("idxt", [128, 32], I32))
    OGIA = ctx.enter_context(nc.sbuf_tensor("ogia", [128, 256], I32))
    OGI = ctx.enter_context(nc.sbuf_tensor("ogi", [128, 32], I32))
    WMY = ctx.enter_context(nc.sbuf_tensor("wmy", [128, 32], F32))

    ARF = ctx.enter_context(nc.sbuf_tensor("arf", [128, 9900], F32))
    ARH = ctx.enter_context(nc.sbuf_tensor("arh", [128, 49600], BF16))
    arf, arh = Arena(ARF, 9900, "arf"), Arena(ARH, 49600, "arh")
    print("sbuf remaining after static allocs:", nc.sbuf_bytes_remaining)

    PSA = ctx.enter_context(nc.psum_tensor("psa", [128, 1024], F32))
    PSB = ctx.enter_context(nc.psum_tensor("psb", [128, 1024], F32))
    PSC = ctx.enter_context(nc.psum_tensor("psc", [128, 512], F32))
    PSD = ctx.enter_context(nc.psum_tensor("psd", [128, 512], F32))
    PSE = ctx.enter_context(nc.psum_tensor("pse", [128, 512], F32))
    PSF = ctx.enter_context(nc.psum_tensor("psf", [128, 512], F32))

    sS = Stream(ctx, nc, "sS")
    sQ = Stream(ctx, nc, "sQ")
    sG = Stream(ctx, nc, "sG")
    sV = Stream(ctx, nc, "sV")
    sA = Stream(ctx, nc, "sA")
    sT = Stream(ctx, nc, "sT")
    sC = Stream(ctx, nc, "sC")
    sO = Stream(ctx, nc, "sO")
    sP = Stream(ctx, nc, "sP")

    block = ctx.enter_context(nc.Block())
    pid_cache = {}

    def pid16(engine):
        if engine not in pid_cache:
            pid_cache[engine] = nc.partition_id(engines=[engine]) * 16
        return pid_cache[engine]

    def prologue():
        @block.sync
        def _(sy):
            for k in range(NT):
                sS.inc(sy.dma_start(out=RES[k],
                                    in_=x_in[128 * k:128 * (k + 1), :]), 16)

        @block.gpsimd
        def _(g):
            g.iota(IOTJ[:, :], pattern=[[1, 128]], base=0, channel_multiplier=0,
                   allow_small_or_imprecise_dtypes=True)
            g.iota(IOTOK[:, :], pattern=[[128, 128]], base=0,
                   channel_multiplier=1, allow_small_or_imprecise_dtypes=True)
            g.iota(IOT8[:, :], pattern=[[1, 8]], base=0, channel_multiplier=0,
                   allow_small_or_imprecise_dtypes=True)
            g.iota(IOTK[:, :], pattern=[[1, 32]], base=0, channel_multiplier=0,
                   allow_small_or_imprecise_dtypes=True)
            g.iota(TRI[:, :], pattern=[[1, 128]], base=0, channel_multiplier=-1,
                   allow_small_or_imprecise_dtypes=True)
            g.tensor_single_scalar(TRI[:, :], TRI[:, :], 0.0, ALU.is_gt)
            g.memset(ONES8[:, :], 1.0)
            g.memset(ONESC[:, :], 1.0)
            g.iota(ID8[:, :], pattern=[[1, 8]], base=0, channel_multiplier=-1,
                   allow_small_or_imprecise_dtypes=True)
            g.tensor_single_scalar(ID8[:, :], ID8[:, :], 0.0, ALU.is_equal)
            g.memset(CST[:, 0:1], 1e-5)
            g.memset(CST[:, 1:2], -1e6)
            g.memset(CST[:, 2:3], 1e-6)
            g.memset(CST[:, 3:4], float(CAP))
            g.memset(CST[:, 4:5], float(CAP - 1))
            g.memset(CST[:, 5:6], 0.5)
            sG.inc(g.dma_start(out=ESELB[:, :],
                               in_=bass.AP(es_in.tensor, 0, [[0, 128], [1, 8]])), 16)
            sG.inc(g.dma_start(out=B2B[:, :],
                               in_=bass.AP(b2_in.tensor, 0, [[0, 128], [1, D]])), 16)
            for f in range(NF):
                sG.inc(g.dma_start(out=B1P[:, f:f + 1],
                                   in_=b1_in[128 * f:128 * (f + 1)]), 16)
            sG.inc(g.dma_start(out=WRS[:, :],
                               in_=bass.AP(wr_in.tensor, 0,
                                           [[8, 128], [1024, 8], [1, 8]])), 16)
            sG.inc(g.dma_start(out=BR8[:, :], in_=br_in[:]), 16)
            sG.inc(g.dma_start(out=BRB[:, :],
                               in_=bass.AP(br_in.tensor, 0, [[0, 128], [1, 8]])), 16)
            for i, m in enumerate((mixk, mixv, mixr)):
                for ci2 in range(ND):
                    sG.inc(g.dma_start(
                        out=MIX[:, 16 * i + ci2:16 * i + ci2 + 1],
                        in_=m[128 * ci2:128 * (ci2 + 1)]), 16)
            sG.inc(g.dma_start(out=EWU[:, 0:1], in_=td_in[:]), 16)
            sG.inc(g.dma_start(out=EWU[:, 1:2], in_=tf_in[:]), 16)
            sG.wait(g)
            for i in range(3):
                g.tensor_scalar(MIX[:, 16 * i + 8:16 * i + 16],
                                MIX[:, 16 * i:16 * i + 8], -1.0, 1.0,
                                ALU.mult, ALU.add)
            sP.inc(g.memset(EWP[:, 16:17], 0.0))

        arh.push()
        cb = arh.take(4096)

        @block.gpsimd
        def _(g):
            jobs = [(w1_in, w1b_d, D, FF), (w2_in, w2b_d, FF, D),
                    (Wk_in, wtb_d[0], D, D), (Wv_in, wtb_d[1], D, D),
                    (Wr_in, wtb_d[2], D, D), (Wo_in, wtb_d[3], D, D)]
            for (src, dst, rows, cols) in jobs:
                for r0 in range(0, rows, 128):
                    sG.inc(g.dma_start(out=cb[:, 0:cols],
                                       in_=src[r0:r0 + 128, :]), 16)
                    sG.wait(g)
                    sG.inc(g.dma_start(out=dst[r0:r0 + 128, :],
                                       in_=cb[:, 0:cols]), 16)
                    sG.wait(g)
        arh.pop()

        @block.scalar
        def _(s):
            sG.wait(s)
            s.activation(out=EWU[:, 2:3], in_=EWU[:, 0:1], func=AF.Exp)
            s.mul(out=EWU[:, 2:3], in_=EWU[:, 2:3], mul=-1.0)
            s.activation(out=EWU[:, 2:3], in_=EWU[:, 2:3], func=AF.Exp)
            sA.inc(s.activation(out=EWU[:, 3:4], in_=EWU[:, 1:2], func=AF.Exp))

        @block.gpsimd
        def _(g):
            sA.wait(g)
            sG.inc(g.dma_start(out=ewu_d[0:1, :], in_=EWU[:, 2:3]), 16)
            sG.inc(g.dma_start(out=ewu_d[1:2, :], in_=EWU[:, 3:4]), 16)
            sG.wait(g)
            for ci in range(ND):
                sG.inc(g.dma_start(out=EWP[:, ci:ci + 1],
                                   in_=bass.AP(ewu_d.tensor, 2 * ci,
                                               [[1, 2], [0, 64]]),
                                   allow_slow_non_contiguous=True), 16)
                sG.inc(g.dma_start(out=EWP[:, 8 + ci:9 + ci],
                                   in_=bass.AP(ewu_d.tensor, H + 2 * ci,
                                               [[1, 2], [0, 64]]),
                                   allow_slow_non_contiguous=True), 16)
            g.iota(EWU[:, 4:5], pattern=[[0, 1]], base=0, channel_multiplier=1,
                   allow_small_or_imprecise_dtypes=True)
            for ci in range(ND):
                g.iota(HMT[:, 128 * ci:128 * (ci + 1)],
                       pattern=[[1, 2], [0, 64]], base=2 * ci,
                       channel_multiplier=0,
                       allow_small_or_imprecise_dtypes=True)
                g.tensor_scalar(HMT[:, 128 * ci:128 * (ci + 1)],
                                HMT[:, 128 * ci:128 * (ci + 1)],
                                EWU[:, 4:5], None, ALU.is_equal)
            g.iota(CST[:, 6:7], pattern=[[0, 1]], base=0, channel_multiplier=1,
                   allow_small_or_imprecise_dtypes=True)
            g.tensor_single_scalar(CST[:, 6:7], CST[:, 6:7], 64.0, ALU.is_ge)
            for ci in range(ND):
                g.iota(HMM[:, 16 * ci:16 * (ci + 1)],
                       pattern=[[1, 16]], base=-2 * ci, channel_multiplier=0,
                       allow_small_or_imprecise_dtypes=True)
                g.tensor_scalar(HMM[:, 16 * ci:16 * (ci + 1)],
                                HMM[:, 16 * ci:16 * (ci + 1)],
                                CST[:, 6:7], None, ALU.is_equal)
            sP.inc(g.memset(CST[:, 6:7], 0.0))

    def ln_phase(wait_v_mark=None):
        arf.push()
        mu = arf.take(1)
        var = arf.take(1)
        aa = arf.take(1)
        bb = arf.take(1)
        sq = arf.take(1024)
        arh.push()
        lnbf = [arh.take(D), arh.take(D)]
        q_marks = []

        for k in range(1 if stage == 101 else NT):
            @block.vector
            def _(v, k=k):
                v = VEng(v)
                if k == 0:
                    sS.wait(v)
                    sG.wait(v)
                    sP.wait(v)
                    if wait_v_mark is not None:
                        sV.wait(v, wait_v_mark)
                t = RES[k]
                v.reduce_sum(mu[:, :], t, axis=AX.X)
                v.tensor_single_scalar(mu[:, :], mu[:, :], 1.0 / D, ALU.mult)
                v.tensor_tensor(sq[:, :], t, t, ALU.mult)
                v.reduce_sum(var[:, :], sq[:, :], axis=AX.X)
                v.tensor_single_scalar(var[:, :], var[:, :], 1.0 / D, ALU.mult)
                v.tensor_tensor(sq[:, 0:1], mu[:, :], mu[:, :], ALU.mult)
                sV.inc(v.tensor_tensor(var[:, :], var[:, :], sq[:, 0:1],
                                       ALU.subtract))

            @block.scalar
            def _(s, k=k):
                sV.wait(s)
                sA.inc(s.activation(out=aa[:, :], in_=var[:, :], func=AF.Sqrt,
                                    bias=CST[:, 0:1], scale=1.0))

            @block.vector
            def _(v, k=k):
                v = VEng(v)
                sA.wait(v)
                v.reciprocal(out=aa[:, :], in_=aa[:, :])
                v.tensor_tensor(bb[:, :], mu[:, :], aa[:, :], ALU.mult)
                v.tensor_single_scalar(bb[:, :], bb[:, :], -1.0, ALU.mult)
                if k >= 2:
                    sQ.wait(v, q_marks[k - 2])
                sV.inc(v.tensor_scalar(lnbf[k % 2][:, :], RES[k], aa[:, 0:1],
                                       bb[:, 0:1], ALU.mult, ALU.add))

            @block.scalar
            def _(s, k=k):
                sV.wait(s)
                sQ.inc(s.dma_start(out=lnx_d[128 * k:128 * (k + 1), :],
                                   in_=lnbf[k % 2][:, :]), 16)
                q_marks.append(sQ.n)
        if stage == 101:
            @block.gpsimd
            def _(g):
                sV.wait(g)
                sA.wait(g)
                sQ.wait(g)
                sO.inc(g.dma_start(out=dbg["dbg_pw"][:, 0:64],
                                   in_=RES[0][:, 0:64]), 16)
                for j, src_ap in enumerate((mu, var, aa, bb)):
                    sO.inc(g.dma_start(
                        out=dbg["dbg_pw"][:, 64 + j:65 + j],
                        in_=src_ap[:, :],
                        allow_slow_non_contiguous=True), 16)
                sO.inc(g.dma_start(out=dbg["dbg_pw"][:, 128:192],
                                   in_=lnbf[0][:, 0:64]), 16)
        arf.pop()
        arh.pop()
        return sQ.n

    def router_ag(lnx_mark):
        arh.push()
        lnfm = [arh.take(T) for _ in range(ND)]
        arf.push()
        lgt = arf.take(2048)
        lgrows = arf.take(NT * 8)

        @block.scalar
        def _(s):
            sQ.wait(s, lnx_mark)
            for ci in range(ND):
                sQ.inc(s.dma_start(out=lnfm[ci],
                                   in_=lnx_d[:, 128 * ci:128 * (ci + 1)],
                                   transpose=True), 16)

        @block.tensor
        def _(t):
            t = TEng(t)
            sQ.wait(t)
            sG.wait(t)
            sP.wait(t)
            for k in range(NT):
                ps = PSB[0:128, 8 * k:8 * (k + 1)]
                for ci in range(ND):
                    t.matmul(ps, lnfm[ci][:, 128 * k:128 * (k + 1)],
                             WRS[:, 8 * ci:8 * ci + 8],
                             start=(ci == 0), stop=(ci == ND - 1))
            sT.inc(t.nop())

        @block.vector
        def _(v):
            v = VEng(v)
            sT.wait(v)
            v.tensor_tensor(
                lgrows[:, :].rearrange("p (k e) -> p k e", e=8),
                PSB[0:128, 0:NT * 8].rearrange("p (k e) -> p k e", e=8),
                BRB[:, :].unsqueeze(1).broadcast_to((128, NT, 8)), ALU.add)
            sV.inc(v.engine_nop())

        @block.gpsimd
        def _(g):
            sV.wait(g)
            sG.inc(g.dma_start(out=lg_d[:, :], in_=lgrows[:, :]), 16)
            sG.wait(g)
            if stage == 102:
                sO.inc(g.dma_start(out=dbg["dbg_pw"][:, 0:128],
                                   in_=lnx_d[0:128, 0:128]), 16)
                sO.inc(g.dma_start(out=dbg["dbg_pw"][:, 128:192],
                                   in_=WRS[:, :]), 16)
                sO.inc(g.dma_start(out=dbg["dbg_pw"][:, 256:384],
                                   in_=lgrows[:, :]), 16)
                sO.inc(g.dma_start(out=dbg["dbg_pw"][:, 384:512],
                                   in_=lnfm[0][:, 0:128]), 16)
            if stage != 102:
                sC.inc(g.collective_compute("AllGather", ALU.bypass,
                                            ins=[lg_d[:, :]], outs=[LG[:, :]],
                                            replica_groups=RG))
                ag_marks["lg"] = sC.n
                sC.inc(g.collective_compute("AllGather", ALU.bypass,
                                            ins=[lnx_d[:, :]],
                                            outs=[LNX[:, :]],
                                            replica_groups=RG))
                ag_marks["lnx"] = sC.n
        arf.pop()
        arh.pop()

    def routing():
        arf.push()
        g1 = arf.take(128)
        g2 = arf.take(128)
        pos0 = arf.take(128)
        pos1 = arf.take(128)
        w0 = arf.take(128)
        w1_ = arf.take(128)
        rcp = arf.take(128)
        een = arf.take(128)
        og0 = arf.take(128)
        og1 = arf.take(128)
        ow0 = arf.take(128)
        ow1 = arf.take(128)
        rank = arf.take(128)
        mine = arf.take(128)
        amat = arf.take(128)
        prs = arf.take(128)
        arf.push()
        gates = arf.take(1024)
        exc = arf.take(1024)
        posa = arf.take(1024)
        tmp = arf.take(1024)
        prefb = arf.take(1024)
        tot = arf.take(2500)
        arh.push()
        m1 = arh.take(1024)
        m2 = arh.take(1024)
        mt = arh.take(1024)

        @block.gpsimd
        def _(g):
            sC.wait(g, ag_marks.get("lg"))
            sG.inc(g.dma_start(
                out=gates[:, :].rearrange("p (c k e) -> p c k e", c=8, e=8),
                in_=bass.AP(LG.tensor, 0, [[128, 128], [16384, 8], [1, 128]])), 16)

        @block.scalar
        def _(s):
            sG.wait(s)
            sA.inc(s.activation(out=gates[:, :], in_=gates[:, :], func=AF.Exp))

        @block.vector
        def _(v):
            v = VEng(v)
            sA.wait(v)
            sP.wait(v)
            v.reduce_max(g1[:, :], grp(gates[:, :]), axis=AX.X)
            v.tensor_tensor(grp(m1[:, :]), grp(gates[:, :]),
                            bcast8(g1[:, :], 8), ALU.is_ge)
            v.scalar_tensor_tensor(tmp[:, :], m1[:, :], CST[:, 1:2],
                                   gates[:, :], ALU.mult, ALU.add)
            v.reduce_max(g2[:, :], grp(tmp[:, :]), axis=AX.X)
            v.tensor_tensor(grp(m2[:, :]), grp(tmp[:, :]),
                            bcast8(g2[:, :], 8), ALU.is_ge)
            sV.inc(v.tensor_tensor(mt[:, :], m1[:, :], m2[:, :], ALU.add))

        @block.tensor
        def _(t):
            t = TEng(t)
            sV.wait(t)
            t.matmul(PSA[:, 0:512], TRI[:, :], mt[:, 0:512])
            t.matmul(PSA[:, 512:1024], TRI[:, :], mt[:, 512:1024])
            t.matmul(PSB[0:1, 0:512], ONESC[:, :], mt[:, 0:512])
            sT.inc(t.matmul(PSB[0:1, 512:1024], ONESC[:, :], mt[:, 512:1024]))

        @block.vector
        def _(v):
            v = VEng(v)
            sT.wait(v)
            v.tensor_copy(out=exc[:, :], in_=PSA[:, :])
            pstep = list(tot.ap[0])[0]
            tot_em = bass.AP(tot.tensor, tot.offset,
                             [[pstep, 1], [1, 128], [128, 8]])
            sV.inc(v.tensor_copy(out=tot_em, in_=grp(PSB[0:1, 0:1024])))

        @block.gpsimd
        def _(g):
            sV.wait(g)
            sG.inc(g.dma_start(
                out=tot[0:8, 1024:1152],
                in_=tot[0:1, 0:1024].rearrange("p (e c) -> p e c", c=128)), 16)

        @block.vector
        def _(v):
            v = VEng(v)
            sG.wait(v)
            v.tensor_tensor_scan(tot[0:8, 1152:1280], ONES8[:, :],
                                 tot[0:8, 1024:1152], 0.0, ALU.mult, ALU.add)
            sV.inc(v.tensor_tensor(tot[0:8, 1280:1408], tot[0:8, 1152:1280],
                                   tot[0:8, 1024:1152], ALU.subtract))

        @block.gpsimd
        def _(g):
            sV.wait(g)
            sG.inc(g.dma_start(
                out=bass.AP(pref_d.tensor, 0, [[1, 1], [128, 8], [1, 128]]),
                in_=tot[0:8, 1280:1408]), 16)
            sG.wait(g)
            sG.inc(g.dma_start(
                out=prefb[:, :],
                in_=bass.AP(pref_d.tensor, 0, [[0, 128], [1, 1024]])), 16)

        @block.vector
        def _(v):
            v = VEng(v)
            sG.wait(v)
            io8 = IOT8[:, :].unsqueeze(1).broadcast_to((128, NCH, 8))
            es8 = ESELB[:, :].unsqueeze(1).broadcast_to((128, NCH, 8))
            pv = bass.AP(prefb.tensor, prefb.offset,
                         [list(prefb.ap[0]), [1, 128], [128, 8]])
            v.tensor_tensor(grp(posa[:, :]), grp(exc[:, :]), pv, ALU.add)
            v.tensor_tensor(grp(tmp[:, :]), grp(m1[:, :]), grp(posa[:, :]),
                            ALU.mult)
            v.reduce_sum(pos0[:, :], grp(tmp[:, :]), axis=AX.X)
            v.tensor_tensor(grp(tmp[:, :]), grp(m2[:, :]), grp(posa[:, :]),
                            ALU.mult)
            v.reduce_sum(pos1[:, :], grp(tmp[:, :]), axis=AX.X)
            v.tensor_scalar(w0[:, :], pos0[:, :], CST[:, 3:4], None, ALU.is_lt)
            v.tensor_tensor(w0[:, :], w0[:, :], g1[:, :], ALU.mult)
            v.tensor_scalar(w1_[:, :], pos1[:, :], CST[:, 3:4], None,
                            ALU.is_lt)
            v.tensor_tensor(w1_[:, :], w1_[:, :], g2[:, :], ALU.mult)
            v.tensor_tensor(rcp[:, :], w0[:, :], w1_[:, :], ALU.add)
            v.tensor_scalar(rcp[:, :], rcp[:, :], CST[:, 2:3], None, ALU.max)
            v.reciprocal(out=rcp[:, :], in_=rcp[:, :])
            v.tensor_scalar(rcp[:, :], rcp[:, :], CST[:, 5:6], None, ALU.mult)
            v.tensor_tensor(ow0[:, :], w0[:, :], rcp[:, :], ALU.mult)
            v.tensor_tensor(ow1[:, :], w1_[:, :], rcp[:, :], ALU.mult)
            v.tensor_tensor(grp(tmp[:, :]), grp(m1[:, :]), io8, ALU.mult)
            v.reduce_sum(een[:, :], grp(tmp[:, :]), axis=AX.X)
            v.tensor_scalar(og0[:, :], pos0[:, :], CST[:, 4:5], None, ALU.min)
            v.scalar_tensor_tensor(og0[:, :], een[:, :], CST[:, 3:4],
                                   og0[:, :], ALU.mult, ALU.add)
            v.tensor_tensor(grp(tmp[:, :]), grp(m2[:, :]), io8, ALU.mult)
            v.reduce_sum(een[:, :], grp(tmp[:, :]), axis=AX.X)
            v.tensor_scalar(og1[:, :], pos1[:, :], CST[:, 4:5], None, ALU.min)
            v.scalar_tensor_tensor(og1[:, :], een[:, :], CST[:, 3:4],
                                   og1[:, :], ALU.mult, ALU.add)
            v.tensor_copy(out=OGIA[:, 0:128], in_=og0[:, :])
            v.tensor_copy(out=OGIA[:, 128:256], in_=og1[:, :])
            v.tensor_tensor(grp(tmp[:, :]), grp(exc[:, :]), es8, ALU.mult)
            v.reduce_sum(rank[:, :], grp(tmp[:, :]), axis=AX.X)
            v.tensor_tensor(grp(tmp[:, :]), grp(mt[:, :]), es8, ALU.mult)
            v.reduce_sum(mine[:, :], grp(tmp[:, :]), axis=AX.X)
            v.tensor_tensor(amat[:, :], mine[:, :], IOTOK[:, :], ALU.mult)
            v.tensor_tensor(grp(tmp[:, :]), pv, es8, ALU.mult)
            v.reduce_sum(prs[:, :], grp(tmp[:, :]), axis=AX.X)
            sV.inc(v.engine_nop())

        @block.gpsimd
        def _(g):
            sV.wait(g)
            sG.inc(g.dma_start(out=ogia_d[:, :], in_=OGIA[:, :]), 16)
            sG.inc(g.dma_start(out=oww_d[:, 0:128], in_=ow0), 16)
            sG.inc(g.dma_start(out=oww_d[:, 128:256], in_=ow1), 16)

        @block.sync
        def _(sy):
            sG.wait(sy)
            p = pid16(ET.SP)
            for (dst, dsrc, half) in ((OGI[:, 0:16], ogia_d, 0),
                                      (OGI[:, 16:32], ogia_d, 1),
                                      (WMY[:, 0:16], oww_d, 0),
                                      (WMY[:, 16:32], oww_d, 1)):
                sS.inc(sy.dma_start(
                    out=dst,
                    in_=bass.AP(dsrc.tensor, p + 128 * half,
                                [[256, 128], [1, 16]])), 16)

        if stage == 1:
            @block.scalar
            def _(s):
                sS.wait(s)
                for i, src in enumerate((pos0, pos1, w0, w1_, ow0, ow1,
                                         rank, mine)):
                    sO.inc(s.dma_start(
                        out=dbg["dbg_pw"][:, 128 * i:128 * (i + 1)],
                        in_=src), 16)
        if stage == 104:
            @block.gpsimd
            def _(s):
                sS.wait(s)
                for i, ap in enumerate((gates[:, 0:128], g1[:, :],
                                        m1[:, 0:128], m2[:, 0:128])):
                    sO.inc(s.dma_start(
                        out=dbg["dbg_pw"][:, 128 * i:128 * (i + 1)],
                        in_=ap), 16)
                sO.inc(s.dma_start(out=dbg["dbg_pw"][:, 512:640],
                                   in_=lg_d[:, :]), 16)
                sO.inc(s.dma_start(out=dbg["dbg_pw"][:, 640:768],
                                   in_=LG[0:128, :]), 16)
                sO.inc(s.dma_start(out=dbg["dbg_pw"][:, 768:896],
                                   in_=LG[128:256, :]), 16)
                sO.inc(s.dma_start(out=dbg["dbg_pw"][:, 896:1024],
                                   in_=LG[896:1024, :]), 16)

        if stage == 104:
            arf.pop()
            arh.pop()
            arf.pop()
            return sV.n
        arf.pop()
        arh.pop()
        arf.push()
        a4 = [arf.take(4), arf.take(4)]
        gch = [arf.take(128), arf.take(128)]
        stg4 = arf.take(4096)
        t_marks = []
        v_marks = []
        ev_marks = {}

        for ch in range(NCH):
            par = ch % 2

            @block.vector
            def _(v, ch=ch, par=par):
                v = VEng(v)
                if ch >= 2:
                    sT.wait(v, t_marks[ch - 2])
                v.memset(a4[par][:, :], 0.0)
                v.tensor_copy(out=a4[par][:, ch % 4:ch % 4 + 1],
                              in_=amat[:, ch:ch + 1])
                v.tensor_scalar(gch[par][:, :], IOTJ[:, :],
                                rank[:, ch:ch + 1], None, ALU.is_equal)
                sV.inc(v.engine_nop())
                v_marks.append(sV.n)

            @block.tensor
            def _(t, ch=ch, par=par):
                t = TEng(t)
                sV.wait(t, v_marks[ch])
                g = ch // 4
                if ch in (64, 96):
                    sV.wait(t, ev_marks[ch // 32 - 2])
                ps = (PSA if (g // 8) % 2 == 0 else PSB)[0:4, :]
                t.matmul(ps[:, 128 * (g % 8):128 * (g % 8) + 128],
                         a4[par][:, :], gch[par][:, :],
                         start=(ch % 4 == 0), stop=(ch % 4 == 3))
                sT.inc(t.nop())
                t_marks.append(sT.n)

            if ch % 32 == 31:
                @block.vector
                def _(v, ch=ch):
                    v = VEng(v)
                    sT.wait(v)
                    b = ch // 32
                    ps = (PSA if (b % 2) == 0 else PSB)[0:4, :]
                    sV.inc(v.tensor_copy(
                        out=stg4[0:4, 1024 * b:1024 * (b + 1)], in_=ps))
                    ev_marks[ch // 32] = sV.n

        @block.gpsimd
        def _(g):
            sV.wait(g)
            sG.inc(g.dma_start(
                out=bass.AP(stag_d.tensor, 0,
                            [[128, 4], [4096, 4], [512, 8], [1, 128]]),
                in_=stg4[0:4, :].rearrange("p (b g j) -> p b g j", g=8, j=128)),
                16)

        arf.pop()
        arf.push()
        shf = [arf.take(256), arf.take(256)]
        prp = arf.take(4)
        qlo = arf.take(32)
        qhi = arf.take(32)

        @block.gpsimd
        def _(g):
            sG.wait(g)
            g.memset(shf[0][:, :], 0.0)
            sG.inc(g.dma_start(out=shf[0][:, 0:128], in_=stag_d[:, :]), 16)
            sG.inc(g.dma_start(out=prp[:, 0:1], in_=prs[0:1, 0:128]), 16)

        @block.vector
        def _(v):
            v = VEng(v)
            sG.wait(v)
            v.memset(prp[:, 2:3], 0.0)
            for qb in (64, 32, 16, 8, 4, 2, 1):
                v.tensor_single_scalar(prp[:, 1:2], prp[:, 0:1],
                                       float(128 * qb), ALU.is_ge)
                v.scalar_tensor_tensor(prp[:, 0:1], prp[:, 1:2],
                                       float(-128 * qb), prp[:, 0:1],
                                       ALU.mult, ALU.add)
                v.scalar_tensor_tensor(prp[:, 2:3], prp[:, 1:2], float(qb),
                                       prp[:, 2:3], ALU.mult, ALU.add)
            v.tensor_scalar(qlo[:, :], IOTK[:, :], prp[:, 2:3], None,
                            ALU.is_equal)
            v.tensor_single_scalar(prp[:, 2:3], prp[:, 2:3], 1.0, ALU.add)
            v.tensor_scalar(qhi[:, :], IOTK[:, :], prp[:, 2:3], None,
                            ALU.is_equal)
            cur = 0
            for bit in (64, 32, 16, 8, 4, 2, 1):
                a, b = shf[cur], shf[1 - cur]
                v.tensor_single_scalar(prp[:, 1:2], prp[:, 0:1], float(bit),
                                       ALU.is_ge)
                v.scalar_tensor_tensor(prp[:, 0:1], prp[:, 1:2], float(-bit),
                                       prp[:, 0:1], ALU.mult, ALU.add)
                v.tensor_scalar(prp[:, 3:4], prp[:, 1:2], -1.0, 1.0,
                                ALU.mult, ALU.add)
                v.tensor_scalar(b[:, :], a[:, :], prp[:, 3:4], None, ALU.mult)
                v.scalar_tensor_tensor(b[:, bit:256], a[:, 0:256 - bit],
                                       prp[:, 1:2], b[:, bit:256],
                                       ALU.mult, ALU.add)
                cur = 1 - cur
            sV.inc(v.engine_nop())

        @block.tensor
        def _(t):
            t = TEng(t)
            sV.wait(t)
            t.matmul(PSE[0:128, 0:32], shf[1][:, 0:128], qlo[:, :],
                     start=True, stop=False)
            sT.inc(t.matmul(PSE[0:128, 0:32], shf[1][:, 128:256], qhi[:, :],
                            start=False, stop=True))

        @block.vector
        def _(v):
            v = VEng(v)
            sT.wait(v)
            sV.inc(v.tensor_copy(out=IDXT[:, :], in_=PSE[0:128, 0:32]))

        if stage == 1:
            @block.scalar
            def _(s):
                sV.wait(s)
                sO.inc(s.dma_start(out=dbg["dbg_slot"][:, :], in_=IDXT[:, :]),
                       16)


        arf.pop()
        arf.pop()
        return sV.n

    def moe_compute():
        arh.push()
        W2T = [arh.take(D) for _ in range(NF)]
        bufr = [arh.take(D), arh.take(D)]
        bufq = [[arh.take(SB) for _ in range(ND)] for _ in range(2)]
        h_all = arh.take(SB * NF)
        outbf = [arh.take(D), arh.take(D)]

        @block.sync
        def _(sy):
            sV.wait(sy)
            for d in range(ND):
                sS.inc(sy.dma_start(out=W1V[d],
                                    in_=w1b_d[128 * d:128 * (d + 1), :]), 16)
            for f in range(NF):
                sS.inc(sy.dma_start(out=W2T[f],
                                    in_=w2b_d[128 * f:128 * (f + 1), :]), 16)

        gmarks = []

        @block.gpsimd
        def _(g):
            sV.wait(g)
            sC.wait(g)
            for sb in range(CAP // 128):
                if sb >= 2:
                    sG.wait(g, gmarks[sb - 2])
                sG.inc(g.indirect_dma_start(
                    bufr[sb % 2][:, :], None, LNX[:, :],
                    IndirectOffsetOnAxis(IDXT[:, sb:sb + 1], 0)), 16)
                sG.wait(g)
                sG.inc(g.dma_start(out=buf_d[128 * sb:128 * (sb + 1), :],
                                   in_=bufr[sb % 2][:, :]), 16)
                gmarks.append(sG.n)

        if stage in (2, 21, 22):
            @block.gpsimd
            def _(g):
                sG.wait(g)
                for r in range(0, CAP, 128):
                    sG.inc(g.dma_start(out=dbg["dbg_buf"][r:r + 128, :],
                                       in_=buf_d[r:r + 128, :]), 16)
        if stage == 21:
            arh.pop()
            return

        hb_marks = {}
        ffn2_marks = {}
        og_marks = []
        for sb in range(5 if stage == 23 else NSB):
            @block.scalar
            def _(s, sb=sb):
                if sb == 0:
                    sG.wait(s, gmarks[1])
                    sS.wait(s)
                else:
                    sG.wait(s, gmarks[2 * sb + 1])
                for d in range(ND):
                    sQ.inc(s.dma_start(
                        out=bufq[sb % 2][d],
                        in_=buf_d[SB * sb:SB * (sb + 1),
                                  128 * d:128 * (d + 1)],
                        transpose=True), 16)

            for f in range(NF):
                @block.tensor
                def _(t, sb=sb, f=f):
                    t = TEng(t)
                    if f == 0:
                        sQ.wait(t)
                    if f >= 2:
                        sA.wait(t, hb_marks[(sb, f - 2)])
                    hp = (PSC if f % 2 == 0 else PSD)[:, 0:SB]
                    for d in range(ND):
                        t.matmul(hp, W1V[d][:, 128 * f:128 * (f + 1)],
                                 bufq[sb % 2][d],
                                 start=(d == 0), stop=(d == ND - 1))
                    sT.inc(t.nop())

                @block.scalar
                def _(s, sb=sb, f=f):
                    sT.wait(s)
                    if f == 0 and sb >= 1:
                        sT.wait(s, ffn2_marks[sb - 1])
                    hp = (PSC if f % 2 == 0 else PSD)[:, 0:SB]
                    sA.inc(s.activation(out=h_all[:, SB * f:SB * (f + 1)],
                                        in_=hp,
                                        func=AF.Gelu, bias=B1P[:, f:f + 1],
                                        scale=1.0))
                    hb_marks[(sb, f)] = sA.n

            @block.tensor
            def _(t, sb=sb):
                t = TEng(t)
                sA.wait(t, hb_marks[(sb, NF - 1)])
                if sb >= 1:
                    sV.wait(t)
                for th, PS in ((0, PSA), (1, PSB)):
                    for hh in range(2):
                        for f in range(NF):
                            t.matmul(PS[:, 512 * hh:512 * (hh + 1)],
                                     h_all[:, SB * f + 128 * th:
                                           SB * f + 128 * (th + 1)],
                                     W2T[f][:, 512 * hh:512 * (hh + 1)],
                                     start=(f == 0), stop=(f == NF - 1))
                sT.inc(t.nop())
                ffn2_marks[sb] = sT.n

            @block.vector
            def _(v, sb=sb):
                v = VEng(v)
                sT.wait(v, ffn2_marks[sb])
                if sb >= 1:
                    sG.wait(v, og_marks[sb - 1])
                v.tensor_tensor(outbf[0][:, :], PSA[:, :], B2B[:, :], ALU.add)
                v.tensor_tensor(outbf[1][:, :], PSB[:, :], B2B[:, :], ALU.add)
                sV.inc(v.engine_nop())

            @block.gpsimd
            def _(g, sb=sb):
                sV.wait(g)
                sG.inc(g.dma_start(
                    out=oute_d[SB * sb:SB * sb + 128, :],
                    in_=outbf[0][:, :]), 16)
                sG.inc(g.dma_start(
                    out=oute_d[SB * sb + 128:SB * (sb + 1), :],
                    in_=outbf[1][:, :]), 16)
                og_marks.append(sG.n)

        if stage in (2, 22, 23):
            @block.gpsimd
            def _(g):
                sG.wait(g)
                for r in range(0, CAP, 128):
                    sG.inc(g.dma_start(out=dbg["dbg_oute"][r:r + 128, :],
                                       in_=oute_d[r:r + 128, :]), 16)
        if stage in (22, 23):
            arh.pop()
            return

        @block.gpsimd
        def _(g):
            sG.wait(g)
            sC.inc(g.collective_compute("AllGather", ALU.bypass,
                                        ins=[oute_d[:, :]], outs=[OUTA[:, :]],
                                        replica_groups=RG))
        arh.pop()

    def combine(src_dram, to_y):
        arh.push()
        ogr = [arh.take(D) for _ in range(4)]

        @block.sync
        def _(sy):
            sT.wait(sy)
            for k in range(NT):
                sS.inc(sy.dma_start(
                    out=RES[k], in_=src_dram[128 * k:128 * (k + 1), :]), 16)

        cmarks = []
        for k in range(NT):
            @block.gpsimd
            def _(g, k=k):
                if k == 0:
                    sC.wait(g)
                if k >= 2:
                    sV.wait(g, cmarks[k - 2])
                sG.inc(g.indirect_dma_start(
                    ogr[2 * (k % 2)][:, :], None, OUTA[:, :],
                    IndirectOffsetOnAxis(OGI[:, k:k + 1], 0)), 16)
                sG.inc(g.indirect_dma_start(
                    ogr[2 * (k % 2) + 1][:, :], None, OUTA[:, :],
                    IndirectOffsetOnAxis(OGI[:, 16 + k:17 + k], 0)), 16)

            @block.vector
            def _(v, k=k):
                v = VEng(v)
                sG.wait(v)
                if k == 0:
                    sS.wait(v)
                v.scalar_tensor_tensor(RES[k], ogr[2 * (k % 2)][:, :],
                                       WMY[:, k:k + 1], RES[k],
                                       ALU.mult, ALU.add)
                v.scalar_tensor_tensor(RES[k], ogr[2 * (k % 2) + 1][:, :],
                                       WMY[:, 16 + k:17 + k], RES[k],
                                       ALU.mult, ALU.add)
                sV.inc(v.engine_nop())
                cmarks.append(sV.n)

            if to_y:
                @block.scalar
                def _(s, k=k):
                    sV.wait(s)
                    sO.inc(s.dma_start(out=y_out[128 * k:128 * (k + 1), :],
                                       in_=RES[k]), 16)
        arh.pop()
        return sV.n

    def attn(lnx_mark):
        arh.push()
        WT = {p: [arh.take(D) for _ in range(ND)] for p in "kvr"}
        WoT = [arh.take(D) for _ in range(ND)]
        lnp = [arh.take(TC) for _ in range(ND)]
        pcol = arh.take(ND)
        xm = [arh.take(TC) for _ in range(ND)]
        vbf = [arh.take(TC) for _ in range(ND)]
        rbf = [[arh.take(TC) for _ in range(ND)] for _ in range(2)]
        ekb = [arh.take(TC) for _ in range(ND)]
        numb = [arh.take(TC) for _ in range(ND)]
        rwkv = [arh.take(TC) for _ in range(ND)]
        rcb = arh.take(TC)
        arf.push()
        ekf = arf.take(TC)
        scin = arf.take(TC + 1)
        saout = arf.take(TC + 1)
        ewt8 = [arf.take(TC + 1) for _ in range(ND)]
        carry = arf.take(ND)
        fw = arf.take(3 * TC + 2 * (TC + 1) + 2)
        o_eks = 0
        o_sbin = TC
        o_sbout = 2 * TC + 1
        o_den = 3 * TC + 2
        o_ewt16 = 4 * TC + 2
        o_sbc = 5 * TC + 3
        lm = {}

        @block.scalar
        def _(s):
            for p, wd in (("k", wtb_d[0]), ("v", wtb_d[1]), ("r", wtb_d[2])):
                for i in range(ND):
                    sQ.inc(s.dma_start(out=WT[p][i],
                                       in_=wd[:, 128 * i:128 * (i + 1)],
                                       transpose=True), 16)
            for i in range(ND):
                sQ.inc(s.dma_start(out=WoT[i],
                                   in_=wtb_d[3][:, 128 * i:128 * (i + 1)],
                                   transpose=True), 16)
            lm["wt"] = sQ.n

        @block.vector
        def _(v):
            v = VEng(v)
            v.memset(carry[:, :], 0.0)
            v.memset(fw[0:16, o_sbc:o_sbc + 1], 0.0)
            v.memset(fw[0:16, o_ewt16:o_ewt16 + TC + 1], 0.0)
            for ci in range(ND):
                v.memset(ewt8[ci][:, :], 0.0)
                v.memset(ewt8[ci][:, 0:1], 1.0)
            v.memset(fw[0:16, o_ewt16:o_ewt16 + 1], 1.0)
            sV.inc(v.engine_nop())

        @block.scalar
        def _(s):
            sV.wait(s)
            s.activation(out=fw[0:16, o_ewt16 + 1:o_ewt16 + 1 + TC],
                         in_=fw[0:16, o_ewt16 + 1:o_ewt16 + 1 + TC],
                         func=AF.Identity, bias=EWU[:, 2:3], scale=0.0)
            for ci in range(ND):
                s.activation(out=ewt8[ci][:, 1:TC + 1],
                             in_=ewt8[ci][:, 1:TC + 1],
                             func=AF.Identity, bias=EWP[:, ci:ci + 1],
                             scale=0.0)
            sA.inc(s.nop())
            lm["ewt"] = sA.n

        def emit_rcbwo(tc):
            for ci in range(ND):
                @block.tensor
                def _(t, tc=tc, ci=ci):
                    t = TEng(t)
                    sV.wait(t, lm[("den", tc)])
                    if ci >= 2:
                        sV.wait(t, lm[("rw", tc, ci - 2)])
                    ps = (PSE if ci % 2 == 0 else PSF)[0:128, 0:TC]
                    t.matmul(ps, HMT[:, 128 * ci:128 * (ci + 1)],
                             rcb[0:16, :])
                    sT.inc(t.nop())
                    lm[("rcb", tc, ci)] = sT.n

                @block.vector
                def _(v, tc=tc, ci=ci):
                    v = VEng(v)
                    sT.wait(v, lm[("rcb", tc, ci)])
                    sV.wait(v, lm[("sa", tc, ND - 1)])
                    sA.wait(v, lm[("pev", tc, "r", ci)])
                    ps = (PSE if ci % 2 == 0 else PSF)[0:128, 0:TC]
                    v.tensor_tensor(ekf[:, :], numb[ci][:, :], ps, ALU.mult)
                    v.tensor_tensor(rwkv[ci][:, :], ekf[:, :],
                                    rbf[tc % 2][ci][:, :], ALU.mult)
                    sV.inc(v.engine_nop())
                    lm[("rw", tc, ci)] = sV.n

            for ts in range(TC // 128):
                @block.tensor
                def _(t, tc=tc, ts=ts):
                    t = TEng(t)
                    sV.wait(t, lm[("rw", tc, ND - 1)])
                    if ts >= 2:
                        sV.wait(t, lm[("wo", tc, ts - 2)])
                    psf = (PSA if ts % 2 == 0 else PSB)
                    for hh in range(2):
                        for ci in range(ND):
                            t.matmul(psf[:, 512 * hh:512 * (hh + 1)],
                                     rwkv[ci][:, 128 * ts:128 * (ts + 1)],
                                     WoT[ci][:, 512 * hh:512 * (hh + 1)],
                                     start=(ci == 0), stop=(ci == ND - 1))
                    sT.inc(t.nop())
                    lm[("womm", tc, ts)] = sT.n

                @block.vector
                def _(v, tc=tc, ts=ts):
                    v = VEng(v)
                    sT.wait(v, lm[("womm", tc, ts)])
                    k = (TC // 128) * tc + ts
                    ps = (PSA if ts % 2 == 0 else PSB)[:, :]
                    sV.inc(v.tensor_tensor(RES[k], RES[k], ps, ALU.add))
                    lm[("wo", tc, ts)] = sV.n


        for tc in range(NTC):
            @block.vector
            def _(v, tc=tc):
                v = VEng(v)
                if tc > 0:
                    sQ.wait(v)
                    for ci in range(ND):
                        v.tensor_copy(out=pcol[:, ci:ci + 1],
                                      in_=lnp[ci][:, TC - 1:TC])
                sV.inc(v.engine_nop())
                lm[("pc", tc)] = sV.n

            @block.scalar
            def _(s, tc=tc):
                sV.wait(s, lm[("pc", tc)])
                if tc == 0:
                    sQ.wait(s, lnx_mark)
                else:
                    sT.wait(s, lm[("pmm", tc - 1, "r", ND - 1)])
                for ci in range(ND):
                    sQ.inc(s.dma_start(
                        out=lnp[ci],
                        in_=lnx_d[TC * tc:TC * (tc + 1),
                                  128 * ci:128 * (ci + 1)],
                        transpose=True), 16)
                lm[("lnp", tc)] = sQ.n

            for ip, p in enumerate("kvr"):
                @block.vector
                def _(v, tc=tc, ip=ip, p=p):
                    v = VEng(v)
                    sQ.wait(v, lm[("lnp", tc)])
                    if ip > 0:
                        prev = "kv"[ip - 1]
                        sT.wait(v, lm[("pmm", tc, prev, ND - 1)])
                    mo = 16 * ip
                    for ci in range(ND):
                        v.tensor_scalar(xm[ci][:, :], lnp[ci][:, :],
                                        MIX[:, mo + ci:mo + ci + 1], None,
                                        ALU.mult)
                        v.scalar_tensor_tensor(
                            xm[ci][:, 1:TC], lnp[ci][:, 0:TC - 1],
                            MIX[:, mo + 8 + ci:mo + 9 + ci], xm[ci][:, 1:TC],
                            ALU.mult, ALU.add)
                        if tc > 0:
                            v.scalar_tensor_tensor(
                                xm[ci][:, 0:1], pcol[:, ci:ci + 1],
                                MIX[:, mo + 8 + ci:mo + 9 + ci],
                                xm[ci][:, 0:1], ALU.mult, ALU.add)
                    sV.inc(v.engine_nop())
                    lm[("xm", tc, p)] = sV.n

                for o in range(ND):
                    @block.tensor
                    def _(t, tc=tc, p=p, o=o):
                        t = TEng(t)
                        if o == 0:
                            sV.wait(t, lm[("xm", tc, p)])
                            if tc == 0 and p == "k":
                                sQ.wait(t, lm["wt"])
                        if o >= 2:
                            sA.wait(t, lm[("pev", tc, p, o - 2)])
                        ps = (PSC if o % 2 == 0 else PSD)[:, 0:TC]
                        for i in range(ND):
                            t.matmul(ps, WT[p][i][:, 128 * o:128 * (o + 1)],
                                     xm[i], start=(i == 0),
                                     stop=(i == ND - 1))
                        sT.inc(t.nop())
                        lm[("pmm", tc, p, o)] = sT.n

                    @block.scalar
                    def _(s, tc=tc, p=p, o=o):
                        sT.wait(s, lm[("pmm", tc, p, o)])
                        ps = (PSC if o % 2 == 0 else PSD)[:, 0:TC]
                        if p == "k":
                            ins = s.activation(out=ekb[o][:, :], in_=ps,
                                               func=AF.Exp)
                        elif p == "v":
                            ins = s.copy(out=vbf[o][:, :], in_=ps)
                        else:
                            ins = s.activation(out=rbf[tc % 2][o][:, :],
                                               in_=ps, func=AF.Sigmoid)
                        sA.inc(ins)
                        lm[("pev", tc, p, o)] = sA.n

            if tc > 0:
                emit_rcbwo(tc - 1)

            for ci in range(ND):
                @block.vector
                def _(v, tc=tc, ci=ci):
                    v = VEng(v)
                    sA.wait(v, lm[("pev", tc, "v", ci)])
                    if tc == 0 and ci == 0:
                        sA.wait(v, lm["ewt"])
                    v.tensor_copy(out=ekf[:, :], in_=ekb[ci][:, :])
                    v.tensor_copy(out=scin[:, 0:1], in_=carry[:, ci:ci + 1])
                    v.tensor_tensor(scin[:, 1:TC + 1], ekf[:, :],
                                    vbf[ci][:, :], ALU.mult)
                    v.tensor_tensor_scan(saout[:, :], ewt8[ci][:, :],
                                         scin[:, :], 0.0, ALU.mult, ALU.add)
                    v.tensor_copy(out=carry[:, ci:ci + 1],
                                  in_=saout[:, TC:TC + 1])
                    v.scalar_tensor_tensor(numb[ci][:, :], scin[:, 1:TC + 1],
                                           EWP[:, 8 + ci:9 + ci],
                                           saout[:, 0:TC], ALU.mult, ALU.add)
                    sV.inc(v.engine_nop())
                    lm[("sa", tc, ci)] = sV.n

            @block.tensor
            def _(t, tc=tc):
                t = TEng(t)
                for ci in range(ND):
                    sA.wait(t, lm[("pev", tc, "k", ci)])
                if tc > 0:
                    sV.wait(t, lm[("rw", tc - 1, ND - 1)])
                for ci in range(ND):
                    t.matmul(PSE[0:16, 0:TC], HMM[:, 16 * ci:16 * (ci + 1)],
                             ekb[ci][:, :], start=(ci == 0),
                             stop=(ci == ND - 1))
                sT.inc(t.nop())
                lm[("eks", tc)] = sT.n

            @block.vector
            def _(v, tc=tc):
                v = VEng(v)
                sT.wait(v, lm[("eks", tc)])
                v.tensor_copy(out=fw[0:16, o_eks:o_eks + TC],
                              in_=PSE[0:16, 0:TC])
                v.tensor_copy(out=fw[0:16, o_sbin:o_sbin + 1],
                              in_=fw[0:16, o_sbc:o_sbc + 1])
                v.tensor_copy(out=fw[0:16, o_sbin + 1:o_sbin + 1 + TC],
                              in_=fw[0:16, o_eks:o_eks + TC])
                v.tensor_tensor_scan(fw[0:16, o_sbout:o_sbout + TC + 1],
                                     fw[0:16, o_ewt16:o_ewt16 + TC + 1],
                                     fw[0:16, o_sbin:o_sbin + TC + 1],
                                     0.0, ALU.mult, ALU.add)
                v.tensor_copy(out=fw[0:16, o_sbc:o_sbc + 1],
                              in_=fw[0:16, o_sbout + TC:o_sbout + TC + 1])
                v.tensor_scalar(fw[0:16, o_den:o_den + TC],
                                fw[0:16, o_eks:o_eks + TC],
                                EWU[:, 3:4], None, ALU.mult)
                v.tensor_scalar(fw[0:16, o_den:o_den + TC],
                                fw[0:16, o_den:o_den + TC],
                                CST[0:16, 2:3], None, ALU.max)
                v.tensor_tensor(fw[0:16, o_den:o_den + TC],
                                fw[0:16, o_den:o_den + TC],
                                fw[0:16, o_sbout:o_sbout + TC], ALU.add)
                v.reciprocal(out=fw[0:16, o_den:o_den + TC],
                             in_=fw[0:16, o_den:o_den + TC])
                if tc > 0:
                    sT.wait(v, lm[("rcb", tc - 1, ND - 1)])
                v.tensor_copy(out=rcb[0:16, :],
                              in_=fw[0:16, o_den:o_den + TC])
                sV.inc(v.engine_nop())
                lm[("den", tc)] = sV.n

        emit_rcbwo(NTC - 1)
        arf.pop()
        arh.pop()
        return sV.n

    def spill_res():
        @block.sync
        def _(sy):
            sV.wait(sy)
            for k in range(NT):
                sS.inc(sy.dma_start(out=x2_d[128 * k:128 * (k + 1), :],
                                    in_=RES[k]), 16)
        return sS.n

    def write_y_from_res():
        @block.scalar
        def _(s):
            sV.wait(s)
            for k in range(NT):
                sO.inc(s.dma_start(out=y_out[128 * k:128 * (k + 1), :],
                                   in_=RES[k]), 16)

    def final_wait():
        @block.scalar
        def _(s):
            sO.wait(s)
            sC.wait(s)
            sG.wait(s)
            sS.wait(s)
            sQ.wait(s)

    prologue()
    if stage == 100:
        write_y_from_res()
        final_wait()
        print("arena peaks: arf", arf.peak, "arh", arh.peak)
        ctx.close()
        return nc
    m_lnx = ln_phase()
    if stage == 101:
        write_y_from_res()
        final_wait()
        print("arena peaks: arf", arf.peak, "arh", arh.peak)
        ctx.close()
        return nc
    router_ag(m_lnx)
    if stage in (102, 103):
        write_y_from_res()
        final_wait()
        print("arena peaks: arf", arf.peak, "arh", arh.peak)
        ctx.close()
        return nc
    routing()
    if stage in (1, 104, 105):
        write_y_from_res()
        final_wait()
    elif stage in (21, 22, 23):
        moe_compute()
        write_y_from_res()
        final_wait()
    elif True:
        moe_compute()
        combine(x_in, to_y=(stage == 2))
        if stage == 2:
            final_wait()
        else:
            m_lnx2 = ln_phase(wait_v_mark=sV.n)
            attn(m_lnx2)
            if stage == 3:
                write_y_from_res()
                final_wait()
            else:
                m_lnx3 = ln_phase(wait_v_mark=sV.n)
                router_ag(m_lnx3)
                spill_res()
                routing()
                moe_compute()
                combine(x2_d, to_y=True)
                final_wait()

    print("arena peaks: arf", arf.peak, "arh", arh.peak)
    ctx.close()
    return nc


def kernel(L0_attn_x, L0_attn_mix_k, L0_attn_mix_v, L0_attn_mix_r,
           L0_attn_Wk, L0_attn_Wv, L0_attn_Wr, L0_attn_Wo,
           L0_attn_time_decay, L0_attn_time_first,
           L0_ffn_w_router, L0_ffn_b_router, L0_ffn_w1, L0_ffn_b1,
           L0_ffn_w2, L0_ffn_b2, _stage=4):
    nc = build_nc(stage=_stage)
    x = np.ascontiguousarray(np.asarray(L0_attn_x, dtype=np.float32))
    w1 = np.ascontiguousarray(np.asarray(L0_ffn_w1, dtype=np.float32))
    b1 = np.ascontiguousarray(np.asarray(L0_ffn_b1, dtype=np.float32))
    w2 = np.ascontiguousarray(np.asarray(L0_ffn_w2, dtype=np.float32))
    b2 = np.ascontiguousarray(np.asarray(L0_ffn_b2, dtype=np.float32))
    reps = {
        "Wk": np.ascontiguousarray(L0_attn_Wk, dtype=np.float32),
        "Wv": np.ascontiguousarray(L0_attn_Wv, dtype=np.float32),
        "Wr": np.ascontiguousarray(L0_attn_Wr, dtype=np.float32),
        "Wo": np.ascontiguousarray(L0_attn_Wo, dtype=np.float32),
        "mixk": np.ascontiguousarray(
            np.asarray(L0_attn_mix_k, dtype=np.float32).reshape(D)),
        "mixv": np.ascontiguousarray(
            np.asarray(L0_attn_mix_v, dtype=np.float32).reshape(D)),
        "mixr": np.ascontiguousarray(
            np.asarray(L0_attn_mix_r, dtype=np.float32).reshape(D)),
        "td": np.ascontiguousarray(L0_attn_time_decay, dtype=np.float32),
        "tf": np.ascontiguousarray(L0_attn_time_first, dtype=np.float32),
        "wrout": np.ascontiguousarray(L0_ffn_w_router, dtype=np.float32),
        "brout": np.ascontiguousarray(L0_ffn_b_router, dtype=np.float32),
    }
    in_maps = []
    for c in range(NCORE):
        m = dict(reps)
        m["x"] = np.ascontiguousarray(x[c])
        m["w1e"] = np.ascontiguousarray(w1[c])
        m["b1e"] = np.ascontiguousarray(b1[c])
        m["w2e"] = np.ascontiguousarray(w2[c])
        m["b2e"] = np.ascontiguousarray(b2[c])
        es = np.zeros(E, np.float32)
        es[c] = 1.0
        m["esel"] = es
        in_maps.append(m)
    res = run_bass_kernel_spmd(nc, in_maps, core_ids=list(range(NCORE)))
    if _stage != 4:
        return res
    return np.stack([res.results[c]["y"] for c in range(NCORE)], axis=0)


# version 06
# speedup vs baseline: 1.0152x; 1.0152x over previous
import numpy as np
from contextlib import ExitStack

import concourse.bass as bass
import concourse.mybir as mybir
from concourse.bass import IndirectOffsetOnAxis
from concourse.bass_utils import run_bass_kernel_spmd
from concourse.replica_groups import maybe_share_collective_output_space

F32 = mybir.dt.float32
BF16 = mybir.dt.bfloat16
I32 = mybir.dt.int32
AF = mybir.ActivationFunctionType
ALU = mybir.AluOpType
AX = mybir.AxisListType
ET = mybir.EngineType

RG = [[0, 1, 2, 3, 4, 5, 6, 7]]
NCORE = 8
T = 2048
NT = T // 128
NTOK = NCORE * T
NCH = NTOK // 128
D = 1024
ND = D // 128
FF = 4096
NF = FF // 128
E = 8
CAP = 2560
SB = 256
NSB = CAP // SB
H = 16
TC = 256
NTC = T // TC


class Stream:
    def __init__(self, ctx, nc, name):
        self.h = ctx.enter_context(nc.semaphore(name))
        self.n = 0

    def inc(self, instr, k=1):
        instr.then_inc(self.h, k)
        self.n += k
        return instr

    def wait(self, eng, n=None):
        eng.wait_ge(self.h, self.n if n is None else n)


class Arena:
    def __init__(self, tensor, n, name):
        self.t, self.n, self.off, self.marks, self.name = tensor, n, 0, [], name
        self.peak = 0

    def take(self, width):
        assert self.off + width <= self.n, \
            f"{self.name} overflow {self.off}+{width}>{self.n}"
        ap = self.t[:, self.off:self.off + width]
        self.off += width
        self.peak = max(self.peak, self.off)
        return ap

    def push(self):
        self.marks.append(self.off)

    def pop(self):
        self.off = self.marks.pop()


_PE_RUN = [0]


class TEng:

    def __init__(self, t):
        self._t = t

    def matmul(self, out, lhsT, rhs, start=None, stop=None, **k):
        s = True if start is None else start
        p = True if stop is None else stop
        r = self._t.matmul(out, lhsT, rhs, start=s, stop=p, **k)
        if not (s and p):
            try:
                w = 1
                for st_, nu in list(rhs.ap)[1:]:
                    w += abs(st_) * (nu - 1)
            except Exception:
                w = 512
            if w > 256:
                n = 5
            elif w > 128:
                n = 3
            elif w > 64:
                n = 2
            else:
                n = 1
            for _ in range(n):
                self._t.nop()
        return r

    def nop(self, *a, **k):
        return self._t.nop(*a, **k)

    def __getattr__(self, name):
        return getattr(self._t, name)


_V_PEND = []
_V_WIN = 4


def _v_span(ap):
    t = getattr(ap, "tensor", None)
    if t is None:
        return None
    key = getattr(t, "name", None) or id(t)
    try:
        off = ap.offset
        dims = list(ap.ap)
        w = 1
        for step, num in dims[1:]:
            w += abs(step) * (num - 1)
        return (key, off, off + w)
    except Exception:
        return (key, -1 << 40, 1 << 40)


class VEng:

    _RED = {"reduce_sum", "reduce_max", "reduce_min", "max",
            "tensor_tensor_scan", "reciprocal"}

    def __init__(self, v):
        self._v = v

    def _raw_nop(self):
        self._v.engine_nop()
        self._age()

    def _age(self):
        keep = []
        for p in _V_PEND:
            p[3] += 1
            if p[3] < _V_WIN:
                keep.append(p)
        _V_PEND[:] = keep

    def _pad_for(self, spans):
        need = 0
        for s in spans:
            if s is None:
                continue
            for p in _V_PEND:
                if p[0] == s[0] and s[1] < p[2] and p[1] < s[2]:
                    need = max(need, _V_WIN - p[3])
        for _ in range(need):
            self._raw_nop()

    def engine_nop(self, *a, **k):
        while _V_PEND:
            self._raw_nop()
        r = self._v.engine_nop(*a, **k)
        self._age()
        return r

    def wait_ge(self, *a, **k):
        r = self._v.wait_ge(*a, **k)
        self._age()
        return r

    def __getattr__(self, name):
        f = getattr(self._v, name)
        if not callable(f):
            return f
        red = name in VEng._RED

        def g(*a, **k):
            wr = k.get("out", None)
            spans = []
            for i, x in enumerate(a):
                if getattr(x, "tensor", None) is None:
                    continue
                if wr is None and not spans and "out" not in k:
                    wr = x
                    continue
                spans.append(_v_span(x))
            for kk, x in k.items():
                if kk != "out" and getattr(x, "tensor", None) is not None:
                    spans.append(_v_span(x))
            wsp = _v_span(wr) if wr is not None else None
            self._pad_for(spans + ([wsp] if wsp is not None else []))
            r = f(*a, **k)
            self._age()
            if wsp is not None:
                _V_PEND.append([wsp[0], wsp[1], wsp[2], 0])
            if red:
                for _ in range(_V_WIN):
                    self._raw_nop()
            return r
        return g


def bcast8(ap, reps):
    return ap.unsqueeze(2).broadcast_to((ap.shape[0], ap.shape[1], reps))


def grp(ap, e=8):
    return ap.rearrange("p (c e) -> p c e", e=e)


def build_nc(stage=4):
    _PE_RUN[0] = 0
    nc = bass.Bass("TRN2", target_bir_lowering=False,
                   detect_race_conditions=False)

    x_in = nc.dram_tensor("x", [T, D], F32, kind="ExternalInput").ap()
    Wk_in = nc.dram_tensor("Wk", [D, D], F32, kind="ExternalInput").ap()
    Wv_in = nc.dram_tensor("Wv", [D, D], F32, kind="ExternalInput").ap()
    Wr_in = nc.dram_tensor("Wr", [D, D], F32, kind="ExternalInput").ap()
    Wo_in = nc.dram_tensor("Wo", [D, D], F32, kind="ExternalInput").ap()
    mixk = nc.dram_tensor("mixk", [D], F32, kind="ExternalInput").ap()
    mixv = nc.dram_tensor("mixv", [D], F32, kind="ExternalInput").ap()
    mixr = nc.dram_tensor("mixr", [D], F32, kind="ExternalInput").ap()
    td_in = nc.dram_tensor("td", [H], F32, kind="ExternalInput").ap()
    tf_in = nc.dram_tensor("tf", [H], F32, kind="ExternalInput").ap()
    wr_in = nc.dram_tensor("wrout", [D, E], F32, kind="ExternalInput").ap()
    br_in = nc.dram_tensor("brout", [E], F32, kind="ExternalInput").ap()
    w1_in = nc.dram_tensor("w1e", [D, FF], F32, kind="ExternalInput").ap()
    b1_in = nc.dram_tensor("b1e", [FF], F32, kind="ExternalInput").ap()
    w2_in = nc.dram_tensor("w2e", [FF, D], F32, kind="ExternalInput").ap()
    b2_in = nc.dram_tensor("b2e", [D], F32, kind="ExternalInput").ap()
    es_in = nc.dram_tensor("esel", [E], F32, kind="ExternalInput").ap()
    y_out = nc.dram_tensor("y", [T, D], F32, kind="ExternalOutput").ap()

    dbg = {}
    ag_marks = {}
    if stage in (1, 101, 102, 104, 105):
        dbg["dbg_pw"] = nc.dram_tensor("dbg_pw", [128, 1024], F32,
                                       kind="ExternalOutput").ap()
    if stage == 1:
        dbg["dbg_slot"] = nc.dram_tensor("dbg_slot", [128, 32], I32,
                                         kind="ExternalOutput").ap()
    if stage in (2, 21, 22, 23):
        dbg["dbg_buf"] = nc.dram_tensor("dbg_buf", [CAP, D], BF16,
                                        kind="ExternalOutput").ap()
    if stage in (2, 22, 23):
        dbg["dbg_oute"] = nc.dram_tensor("dbg_oute", [CAP, D], BF16,
                                         kind="ExternalOutput").ap()

    shr = maybe_share_collective_output_space("AllGather", RG)
    lnx_d = nc.dram_tensor("lnx_d", [T, D], BF16, kind="Internal").ap()
    LNX = nc.dram_tensor("LNX", [NTOK, D], BF16, kind="Internal",
                         addr_space=shr).ap()
    lg_d = nc.dram_tensor("lg_d", [128, 128], F32, kind="Internal").ap()
    LG = nc.dram_tensor("LG", [1024, 128], F32, kind="Internal",
                        addr_space=shr).ap()
    stag_d = nc.dram_tensor("stag_d", [NCH, 128], F32, kind="Internal").ap()
    buf_d = nc.dram_tensor("buf_d", [CAP, D], BF16, kind="Internal").ap()
    oute_d = nc.dram_tensor("oute_d", [CAP, D], BF16, kind="Internal").ap()
    OUTA = nc.dram_tensor("OUTA", [NCORE * CAP, D], BF16, kind="Internal",
                          addr_space=shr).ap()
    w1b_d = nc.dram_tensor("w1b_d", [D, FF], BF16, kind="Internal").ap()
    w2b_d = nc.dram_tensor("w2b_d", [FF, D], BF16, kind="Internal").ap()
    wtb_d = [nc.dram_tensor(f"wtb{i}", [D, D], BF16, kind="Internal").ap()
             for i in range(4)]
    ewu_d = nc.dram_tensor("ewu_d", [2, H], F32, kind="Internal").ap()
    pref_d = nc.dram_tensor("pref_d", [1024], F32, kind="Internal").ap()
    ogia_d = nc.dram_tensor("ogia_d", [128, 256], I32, kind="Internal").ap()
    oww_d = nc.dram_tensor("oww_d", [128, 256], F32, kind="Internal").ap()
    x2_d = nc.dram_tensor("x2_d", [T, D], F32, kind="Internal").ap()

    ctx = ExitStack()

    RESR = ctx.enter_context(nc.sbuf_tensor("resr", [128, NT * D], F32))
    RES = [RESR[:, D * k:D * (k + 1)] for k in range(NT)]
    W1V = [RESR[:, 2048 * d:2048 * (d + 1)].bitcast(BF16) for d in range(ND)]

    TRI = ctx.enter_context(nc.sbuf_tensor("tri", [128, 128], BF16))
    IOTJ = ctx.enter_context(nc.sbuf_tensor("iotj", [128, 128], F32))
    IOTOK = ctx.enter_context(nc.sbuf_tensor("iotok", [128, 128], F32))
    IOT8 = ctx.enter_context(nc.sbuf_tensor("iot8", [128, 8], F32))
    IOTK = ctx.enter_context(nc.sbuf_tensor("iotk", [128, 32], F32))
    ONES8 = ctx.enter_context(nc.sbuf_tensor("ones8", [8, 128], F32))
    ONESC = ctx.enter_context(nc.sbuf_tensor("onesc", [128, 1], BF16))
    ID8 = ctx.enter_context(nc.sbuf_tensor("id8", [8, 8], F32))
    ESELB = ctx.enter_context(nc.sbuf_tensor("eselb", [128, 8], F32))
    B2B = ctx.enter_context(nc.sbuf_tensor("b2b", [128, D], BF16))
    B1P = ctx.enter_context(nc.sbuf_tensor("b1p", [128, NF], F32))
    WRS = ctx.enter_context(nc.sbuf_tensor("wrs", [128, 64], BF16))
    BR8 = ctx.enter_context(nc.sbuf_tensor("br8", [8, 1], F32))
    BRB = ctx.enter_context(nc.sbuf_tensor("brb", [128, 8], F32))
    MIX = ctx.enter_context(nc.sbuf_tensor("mix", [128, 48], F32))
    EWU = ctx.enter_context(nc.sbuf_tensor("ewu", [16, 8], F32))
    EWP = ctx.enter_context(nc.sbuf_tensor("ewp", [128, 17], F32))
    HMT = ctx.enter_context(nc.sbuf_tensor("hmt", [16, 1024], BF16))
    HMM = ctx.enter_context(nc.sbuf_tensor("hmm", [128, 128], BF16))
    CST = ctx.enter_context(nc.sbuf_tensor("cst", [128, 8], F32))
    IDXT = ctx.enter_context(nc.sbuf_tensor("idxt", [128, 32], I32))
    OGIA = ctx.enter_context(nc.sbuf_tensor("ogia", [128, 256], I32))
    OGI = ctx.enter_context(nc.sbuf_tensor("ogi", [128, 32], I32))
    WMY = ctx.enter_context(nc.sbuf_tensor("wmy", [128, 32], F32))

    ARF = ctx.enter_context(nc.sbuf_tensor("arf", [128, 9900], F32))
    ARH = ctx.enter_context(nc.sbuf_tensor("arh", [128, 49600], BF16))
    arf, arh = Arena(ARF, 9900, "arf"), Arena(ARH, 49600, "arh")
    print("sbuf remaining after static allocs:", nc.sbuf_bytes_remaining)

    PSA = ctx.enter_context(nc.psum_tensor("psa", [128, 1024], F32))
    PSB = ctx.enter_context(nc.psum_tensor("psb", [128, 1024], F32))
    PSC = ctx.enter_context(nc.psum_tensor("psc", [128, 512], F32))
    PSD = ctx.enter_context(nc.psum_tensor("psd", [128, 512], F32))
    PSE = ctx.enter_context(nc.psum_tensor("pse", [128, 512], F32))
    PSF = ctx.enter_context(nc.psum_tensor("psf", [128, 512], F32))

    sS = Stream(ctx, nc, "sS")
    sQ = Stream(ctx, nc, "sQ")
    sG = Stream(ctx, nc, "sG")
    sV = Stream(ctx, nc, "sV")
    sA = Stream(ctx, nc, "sA")
    sT = Stream(ctx, nc, "sT")
    sC = Stream(ctx, nc, "sC")
    sO = Stream(ctx, nc, "sO")
    sP = Stream(ctx, nc, "sP")

    block = ctx.enter_context(nc.Block())
    pid_cache = {}

    def pid16(engine):
        if engine not in pid_cache:
            pid_cache[engine] = nc.partition_id(engines=[engine]) * 16
        return pid_cache[engine]

    def prologue():
        @block.sync
        def _(sy):
            for k in range(NT):
                sS.inc(sy.dma_start(out=RES[k],
                                    in_=x_in[128 * k:128 * (k + 1), :]), 16)

        @block.gpsimd
        def _(g):
            g.iota(IOTJ[:, :], pattern=[[1, 128]], base=0, channel_multiplier=0,
                   allow_small_or_imprecise_dtypes=True)
            g.iota(IOTOK[:, :], pattern=[[128, 128]], base=0,
                   channel_multiplier=1, allow_small_or_imprecise_dtypes=True)
            g.iota(IOT8[:, :], pattern=[[1, 8]], base=0, channel_multiplier=0,
                   allow_small_or_imprecise_dtypes=True)
            g.iota(IOTK[:, :], pattern=[[1, 32]], base=0, channel_multiplier=0,
                   allow_small_or_imprecise_dtypes=True)
            g.iota(TRI[:, :], pattern=[[1, 128]], base=0, channel_multiplier=-1,
                   allow_small_or_imprecise_dtypes=True)
            g.tensor_single_scalar(TRI[:, :], TRI[:, :], 0.0, ALU.is_gt)
            g.memset(ONES8[:, :], 1.0)
            g.memset(ONESC[:, :], 1.0)
            g.iota(ID8[:, :], pattern=[[1, 8]], base=0, channel_multiplier=-1,
                   allow_small_or_imprecise_dtypes=True)
            g.tensor_single_scalar(ID8[:, :], ID8[:, :], 0.0, ALU.is_equal)
            g.memset(CST[:, 0:1], 1e-5)
            g.memset(CST[:, 1:2], -1e6)
            g.memset(CST[:, 2:3], 1e-6)
            g.memset(CST[:, 3:4], float(CAP))
            g.memset(CST[:, 4:5], float(CAP - 1))
            g.memset(CST[:, 5:6], 0.5)
            sG.inc(g.dma_start(out=ESELB[:, :],
                               in_=bass.AP(es_in.tensor, 0, [[0, 128], [1, 8]])), 16)
            sG.inc(g.dma_start(out=B2B[:, :],
                               in_=bass.AP(b2_in.tensor, 0, [[0, 128], [1, D]])), 16)
            for f in range(NF):
                sG.inc(g.dma_start(out=B1P[:, f:f + 1],
                                   in_=b1_in[128 * f:128 * (f + 1)]), 16)
            sG.inc(g.dma_start(out=WRS[:, :],
                               in_=bass.AP(wr_in.tensor, 0,
                                           [[8, 128], [1024, 8], [1, 8]])), 16)
            sG.inc(g.dma_start(out=BR8[:, :], in_=br_in[:]), 16)
            sG.inc(g.dma_start(out=BRB[:, :],
                               in_=bass.AP(br_in.tensor, 0, [[0, 128], [1, 8]])), 16)
            for i, m in enumerate((mixk, mixv, mixr)):
                for ci2 in range(ND):
                    sG.inc(g.dma_start(
                        out=MIX[:, 16 * i + ci2:16 * i + ci2 + 1],
                        in_=m[128 * ci2:128 * (ci2 + 1)]), 16)
            sG.inc(g.dma_start(out=EWU[:, 0:1], in_=td_in[:]), 16)
            sG.inc(g.dma_start(out=EWU[:, 1:2], in_=tf_in[:]), 16)
            sG.wait(g)
            for i in range(3):
                g.tensor_scalar(MIX[:, 16 * i + 8:16 * i + 16],
                                MIX[:, 16 * i:16 * i + 8], -1.0, 1.0,
                                ALU.mult, ALU.add)
            sP.inc(g.memset(EWP[:, 16:17], 0.0))

        arh.push()
        cb = arh.take(4096)

        @block.gpsimd
        def _(g):
            jobs = [(w1_in, w1b_d, D, FF), (w2_in, w2b_d, FF, D),
                    (Wk_in, wtb_d[0], D, D), (Wv_in, wtb_d[1], D, D),
                    (Wr_in, wtb_d[2], D, D), (Wo_in, wtb_d[3], D, D)]
            for (src, dst, rows, cols) in jobs:
                for r0 in range(0, rows, 128):
                    sG.inc(g.dma_start(out=cb[:, 0:cols],
                                       in_=src[r0:r0 + 128, :]), 16)
                    sG.wait(g)
                    sG.inc(g.dma_start(out=dst[r0:r0 + 128, :],
                                       in_=cb[:, 0:cols]), 16)
                    sG.wait(g)
        arh.pop()

        @block.scalar
        def _(s):
            sG.wait(s)
            s.activation(out=EWU[:, 2:3], in_=EWU[:, 0:1], func=AF.Exp)
            s.mul(out=EWU[:, 2:3], in_=EWU[:, 2:3], mul=-1.0)
            s.activation(out=EWU[:, 2:3], in_=EWU[:, 2:3], func=AF.Exp)
            sA.inc(s.activation(out=EWU[:, 3:4], in_=EWU[:, 1:2], func=AF.Exp))

        @block.gpsimd
        def _(g):
            sA.wait(g)
            sG.inc(g.dma_start(out=ewu_d[0:1, :], in_=EWU[:, 2:3]), 16)
            sG.inc(g.dma_start(out=ewu_d[1:2, :], in_=EWU[:, 3:4]), 16)
            sG.wait(g)
            for ci in range(ND):
                sG.inc(g.dma_start(out=EWP[:, ci:ci + 1],
                                   in_=bass.AP(ewu_d.tensor, 2 * ci,
                                               [[1, 2], [0, 64]]),
                                   allow_slow_non_contiguous=True), 16)
                sG.inc(g.dma_start(out=EWP[:, 8 + ci:9 + ci],
                                   in_=bass.AP(ewu_d.tensor, H + 2 * ci,
                                               [[1, 2], [0, 64]]),
                                   allow_slow_non_contiguous=True), 16)
            g.iota(EWU[:, 4:5], pattern=[[0, 1]], base=0, channel_multiplier=1,
                   allow_small_or_imprecise_dtypes=True)
            for ci in range(ND):
                g.iota(HMT[:, 128 * ci:128 * (ci + 1)],
                       pattern=[[1, 2], [0, 64]], base=2 * ci,
                       channel_multiplier=0,
                       allow_small_or_imprecise_dtypes=True)
                g.tensor_scalar(HMT[:, 128 * ci:128 * (ci + 1)],
                                HMT[:, 128 * ci:128 * (ci + 1)],
                                EWU[:, 4:5], None, ALU.is_equal)
            g.iota(CST[:, 6:7], pattern=[[0, 1]], base=0, channel_multiplier=1,
                   allow_small_or_imprecise_dtypes=True)
            g.tensor_single_scalar(CST[:, 6:7], CST[:, 6:7], 64.0, ALU.is_ge)
            for ci in range(ND):
                g.iota(HMM[:, 16 * ci:16 * (ci + 1)],
                       pattern=[[1, 16]], base=-2 * ci, channel_multiplier=0,
                       allow_small_or_imprecise_dtypes=True)
                g.tensor_scalar(HMM[:, 16 * ci:16 * (ci + 1)],
                                HMM[:, 16 * ci:16 * (ci + 1)],
                                CST[:, 6:7], None, ALU.is_equal)
            sP.inc(g.memset(CST[:, 6:7], 0.0))

    def ln_phase(wait_v_mark=None):
        arf.push()
        mu = arf.take(1)
        var = arf.take(1)
        aa = arf.take(1)
        bb = arf.take(1)
        sq = arf.take(1024)
        arh.push()
        lnbf = [arh.take(D), arh.take(D)]
        q_marks = []

        for k in range(1 if stage == 101 else NT):
            @block.vector
            def _(v, k=k):
                v = VEng(v)
                if k == 0:
                    sS.wait(v)
                    sG.wait(v)
                    sP.wait(v)
                    if wait_v_mark is not None:
                        sV.wait(v, wait_v_mark)
                t = RES[k]
                v.reduce_sum(mu[:, :], t, axis=AX.X)
                v.tensor_single_scalar(mu[:, :], mu[:, :], 1.0 / D, ALU.mult)
                v.tensor_tensor(sq[:, :], t, t, ALU.mult)
                v.reduce_sum(var[:, :], sq[:, :], axis=AX.X)
                v.tensor_single_scalar(var[:, :], var[:, :], 1.0 / D, ALU.mult)
                v.tensor_tensor(sq[:, 0:1], mu[:, :], mu[:, :], ALU.mult)
                sV.inc(v.tensor_tensor(var[:, :], var[:, :], sq[:, 0:1],
                                       ALU.subtract))

            @block.scalar
            def _(s, k=k):
                sV.wait(s)
                sA.inc(s.activation(out=aa[:, :], in_=var[:, :], func=AF.Sqrt,
                                    bias=CST[:, 0:1], scale=1.0))

            @block.vector
            def _(v, k=k):
                v = VEng(v)
                sA.wait(v)
                v.reciprocal(out=aa[:, :], in_=aa[:, :])
                v.tensor_tensor(bb[:, :], mu[:, :], aa[:, :], ALU.mult)
                v.tensor_single_scalar(bb[:, :], bb[:, :], -1.0, ALU.mult)
                if k >= 2:
                    sQ.wait(v, q_marks[k - 2])
                sV.inc(v.tensor_scalar(lnbf[k % 2][:, :], RES[k], aa[:, 0:1],
                                       bb[:, 0:1], ALU.mult, ALU.add))

            @block.scalar
            def _(s, k=k):
                sV.wait(s)
                sQ.inc(s.dma_start(out=lnx_d[128 * k:128 * (k + 1), :],
                                   in_=lnbf[k % 2][:, :]), 16)
                q_marks.append(sQ.n)
        if stage == 101:
            @block.gpsimd
            def _(g):
                sV.wait(g)
                sA.wait(g)
                sQ.wait(g)
                sO.inc(g.dma_start(out=dbg["dbg_pw"][:, 0:64],
                                   in_=RES[0][:, 0:64]), 16)
                for j, src_ap in enumerate((mu, var, aa, bb)):
                    sO.inc(g.dma_start(
                        out=dbg["dbg_pw"][:, 64 + j:65 + j],
                        in_=src_ap[:, :],
                        allow_slow_non_contiguous=True), 16)
                sO.inc(g.dma_start(out=dbg["dbg_pw"][:, 128:192],
                                   in_=lnbf[0][:, 0:64]), 16)
        arf.pop()
        arh.pop()
        return sQ.n

    def router_ag(lnx_mark):
        arh.push()
        lnfm = [arh.take(T) for _ in range(ND)]
        arf.push()
        lgt = arf.take(2048)
        lgrows = arf.take(NT * 8)

        @block.scalar
        def _(s):
            sQ.wait(s, lnx_mark)
            for ci in range(ND):
                sQ.inc(s.dma_start(out=lnfm[ci],
                                   in_=lnx_d[:, 128 * ci:128 * (ci + 1)],
                                   transpose=True), 16)

        @block.tensor
        def _(t):
            t = TEng(t)
            sQ.wait(t)
            sG.wait(t)
            sP.wait(t)
            for k in range(NT):
                ps = PSB[0:128, 8 * k:8 * (k + 1)]
                for ci in range(ND):
                    t.matmul(ps, lnfm[ci][:, 128 * k:128 * (k + 1)],
                             WRS[:, 8 * ci:8 * ci + 8],
                             start=(ci == 0), stop=(ci == ND - 1))
            sT.inc(t.nop())

        @block.vector
        def _(v):
            v = VEng(v)
            sT.wait(v)
            v.tensor_tensor(
                lgrows[:, :].rearrange("p (k e) -> p k e", e=8),
                PSB[0:128, 0:NT * 8].rearrange("p (k e) -> p k e", e=8),
                BRB[:, :].unsqueeze(1).broadcast_to((128, NT, 8)), ALU.add)
            sV.inc(v.engine_nop())

        @block.gpsimd
        def _(g):
            sV.wait(g)
            sG.inc(g.dma_start(out=lg_d[:, :], in_=lgrows[:, :]), 16)
            sG.wait(g)
            if stage == 102:
                sO.inc(g.dma_start(out=dbg["dbg_pw"][:, 0:128],
                                   in_=lnx_d[0:128, 0:128]), 16)
                sO.inc(g.dma_start(out=dbg["dbg_pw"][:, 128:192],
                                   in_=WRS[:, :]), 16)
                sO.inc(g.dma_start(out=dbg["dbg_pw"][:, 256:384],
                                   in_=lgrows[:, :]), 16)
                sO.inc(g.dma_start(out=dbg["dbg_pw"][:, 384:512],
                                   in_=lnfm[0][:, 0:128]), 16)
            if stage != 102:
                sC.inc(g.collective_compute("AllGather", ALU.bypass,
                                            ins=[lg_d[:, :]], outs=[LG[:, :]],
                                            replica_groups=RG))
                ag_marks["lg"] = sC.n
                sC.inc(g.collective_compute("AllGather", ALU.bypass,
                                            ins=[lnx_d[:, :]],
                                            outs=[LNX[:, :]],
                                            replica_groups=RG))
                ag_marks["lnx"] = sC.n
        arf.pop()
        arh.pop()

    def routing():
        arf.push()
        g1 = arf.take(128)
        g2 = arf.take(128)
        pos0 = arf.take(128)
        pos1 = arf.take(128)
        w0 = arf.take(128)
        w1_ = arf.take(128)
        rcp = arf.take(128)
        een = arf.take(128)
        og0 = arf.take(128)
        og1 = arf.take(128)
        ow0 = arf.take(128)
        ow1 = arf.take(128)
        rank = arf.take(128)
        mine = arf.take(128)
        amat = arf.take(128)
        prs = arf.take(128)
        arf.push()
        gates = arf.take(1024)
        exc = arf.take(1024)
        posa = arf.take(1024)
        tmp = arf.take(1024)
        prefb = arf.take(1024)
        tot = arf.take(2500)
        arh.push()
        m1 = arh.take(1024)
        m2 = arh.take(1024)
        mt = arh.take(1024)

        @block.gpsimd
        def _(g):
            sC.wait(g, ag_marks.get("lg"))
            sG.inc(g.dma_start(
                out=gates[:, :].rearrange("p (c k e) -> p c k e", c=8, e=8),
                in_=bass.AP(LG.tensor, 0, [[128, 128], [16384, 8], [1, 128]])), 16)

        @block.scalar
        def _(s):
            sG.wait(s)
            sA.inc(s.activation(out=gates[:, :], in_=gates[:, :], func=AF.Exp))

        @block.vector
        def _(v):
            v = VEng(v)
            sA.wait(v)
            sP.wait(v)
            v.reduce_max(g1[:, :], grp(gates[:, :]), axis=AX.X)
            v.tensor_tensor(grp(m1[:, :]), grp(gates[:, :]),
                            bcast8(g1[:, :], 8), ALU.is_ge)
            v.scalar_tensor_tensor(tmp[:, :], m1[:, :], CST[:, 1:2],
                                   gates[:, :], ALU.mult, ALU.add)
            v.reduce_max(g2[:, :], grp(tmp[:, :]), axis=AX.X)
            v.tensor_tensor(grp(m2[:, :]), grp(tmp[:, :]),
                            bcast8(g2[:, :], 8), ALU.is_ge)
            sV.inc(v.tensor_tensor(mt[:, :], m1[:, :], m2[:, :], ALU.add))

        @block.tensor
        def _(t):
            t = TEng(t)
            sV.wait(t)
            t.matmul(PSA[:, 0:512], TRI[:, :], mt[:, 0:512])
            t.matmul(PSA[:, 512:1024], TRI[:, :], mt[:, 512:1024])
            t.matmul(PSB[0:1, 0:512], ONESC[:, :], mt[:, 0:512])
            sT.inc(t.matmul(PSB[0:1, 512:1024], ONESC[:, :], mt[:, 512:1024]))

        @block.vector
        def _(v):
            v = VEng(v)
            sT.wait(v)
            v.tensor_copy(out=exc[:, :], in_=PSA[:, :])
            pstep = list(tot.ap[0])[0]
            tot_em = bass.AP(tot.tensor, tot.offset,
                             [[pstep, 1], [1, 128], [128, 8]])
            sV.inc(v.tensor_copy(out=tot_em, in_=grp(PSB[0:1, 0:1024])))

        @block.gpsimd
        def _(g):
            sV.wait(g)
            sG.inc(g.dma_start(
                out=tot[0:8, 1024:1152],
                in_=tot[0:1, 0:1024].rearrange("p (e c) -> p e c", c=128)), 16)

        @block.vector
        def _(v):
            v = VEng(v)
            sG.wait(v)
            v.tensor_tensor_scan(tot[0:8, 1152:1280], ONES8[:, :],
                                 tot[0:8, 1024:1152], 0.0, ALU.mult, ALU.add)
            sV.inc(v.tensor_tensor(tot[0:8, 1280:1408], tot[0:8, 1152:1280],
                                   tot[0:8, 1024:1152], ALU.subtract))

        @block.gpsimd
        def _(g):
            sV.wait(g)
            sG.inc(g.dma_start(
                out=bass.AP(pref_d.tensor, 0, [[1, 1], [128, 8], [1, 128]]),
                in_=tot[0:8, 1280:1408]), 16)
            sG.wait(g)
            sG.inc(g.dma_start(
                out=prefb[:, :],
                in_=bass.AP(pref_d.tensor, 0, [[0, 128], [1, 1024]])), 16)

        @block.vector
        def _(v):
            v = VEng(v)
            sG.wait(v)
            io8 = IOT8[:, :].unsqueeze(1).broadcast_to((128, NCH, 8))
            es8 = ESELB[:, :].unsqueeze(1).broadcast_to((128, NCH, 8))
            pv = bass.AP(prefb.tensor, prefb.offset,
                         [list(prefb.ap[0]), [1, 128], [128, 8]])
            v.tensor_tensor(grp(posa[:, :]), grp(exc[:, :]), pv, ALU.add)
            v.tensor_tensor(grp(tmp[:, :]), grp(m1[:, :]), grp(posa[:, :]),
                            ALU.mult)
            v.reduce_sum(pos0[:, :], grp(tmp[:, :]), axis=AX.X)
            v.tensor_tensor(grp(tmp[:, :]), grp(m2[:, :]), grp(posa[:, :]),
                            ALU.mult)
            v.reduce_sum(pos1[:, :], grp(tmp[:, :]), axis=AX.X)
            v.tensor_scalar(w0[:, :], pos0[:, :], CST[:, 3:4], None, ALU.is_lt)
            v.tensor_tensor(w0[:, :], w0[:, :], g1[:, :], ALU.mult)
            v.tensor_scalar(w1_[:, :], pos1[:, :], CST[:, 3:4], None,
                            ALU.is_lt)
            v.tensor_tensor(w1_[:, :], w1_[:, :], g2[:, :], ALU.mult)
            v.tensor_tensor(rcp[:, :], w0[:, :], w1_[:, :], ALU.add)
            v.tensor_scalar(rcp[:, :], rcp[:, :], CST[:, 2:3], None, ALU.max)
            v.reciprocal(out=rcp[:, :], in_=rcp[:, :])
            v.tensor_scalar(rcp[:, :], rcp[:, :], CST[:, 5:6], None, ALU.mult)
            v.tensor_tensor(ow0[:, :], w0[:, :], rcp[:, :], ALU.mult)
            v.tensor_tensor(ow1[:, :], w1_[:, :], rcp[:, :], ALU.mult)
            v.tensor_tensor(grp(tmp[:, :]), grp(m1[:, :]), io8, ALU.mult)
            v.reduce_sum(een[:, :], grp(tmp[:, :]), axis=AX.X)
            v.tensor_scalar(og0[:, :], pos0[:, :], CST[:, 4:5], None, ALU.min)
            v.scalar_tensor_tensor(og0[:, :], een[:, :], CST[:, 3:4],
                                   og0[:, :], ALU.mult, ALU.add)
            v.tensor_tensor(grp(tmp[:, :]), grp(m2[:, :]), io8, ALU.mult)
            v.reduce_sum(een[:, :], grp(tmp[:, :]), axis=AX.X)
            v.tensor_scalar(og1[:, :], pos1[:, :], CST[:, 4:5], None, ALU.min)
            v.scalar_tensor_tensor(og1[:, :], een[:, :], CST[:, 3:4],
                                   og1[:, :], ALU.mult, ALU.add)
            v.tensor_copy(out=OGIA[:, 0:128], in_=og0[:, :])
            v.tensor_copy(out=OGIA[:, 128:256], in_=og1[:, :])
            v.tensor_tensor(grp(tmp[:, :]), grp(exc[:, :]), es8, ALU.mult)
            v.reduce_sum(rank[:, :], grp(tmp[:, :]), axis=AX.X)
            v.tensor_tensor(grp(tmp[:, :]), grp(mt[:, :]), es8, ALU.mult)
            v.reduce_sum(mine[:, :], grp(tmp[:, :]), axis=AX.X)
            v.tensor_tensor(amat[:, :], mine[:, :], IOTOK[:, :], ALU.mult)
            v.tensor_tensor(grp(tmp[:, :]), pv, es8, ALU.mult)
            v.reduce_sum(prs[:, :], grp(tmp[:, :]), axis=AX.X)
            sV.inc(v.engine_nop())

        @block.gpsimd
        def _(g):
            sV.wait(g)
            sG.inc(g.dma_start(out=ogia_d[:, :], in_=OGIA[:, :]), 16)
            sG.inc(g.dma_start(out=oww_d[:, 0:128], in_=ow0), 16)
            sG.inc(g.dma_start(out=oww_d[:, 128:256], in_=ow1), 16)

        @block.sync
        def _(sy):
            sG.wait(sy)
            p = pid16(ET.SP)
            for (dst, dsrc, half) in ((OGI[:, 0:16], ogia_d, 0),
                                      (OGI[:, 16:32], ogia_d, 1),
                                      (WMY[:, 0:16], oww_d, 0),
                                      (WMY[:, 16:32], oww_d, 1)):
                sS.inc(sy.dma_start(
                    out=dst,
                    in_=bass.AP(dsrc.tensor, p + 128 * half,
                                [[256, 128], [1, 16]])), 16)

        if stage == 1:
            @block.scalar
            def _(s):
                sS.wait(s)
                for i, src in enumerate((pos0, pos1, w0, w1_, ow0, ow1,
                                         rank, mine)):
                    sO.inc(s.dma_start(
                        out=dbg["dbg_pw"][:, 128 * i:128 * (i + 1)],
                        in_=src), 16)
        if stage == 104:
            @block.gpsimd
            def _(s):
                sS.wait(s)
                for i, ap in enumerate((gates[:, 0:128], g1[:, :],
                                        m1[:, 0:128], m2[:, 0:128])):
                    sO.inc(s.dma_start(
                        out=dbg["dbg_pw"][:, 128 * i:128 * (i + 1)],
                        in_=ap), 16)
                sO.inc(s.dma_start(out=dbg["dbg_pw"][:, 512:640],
                                   in_=lg_d[:, :]), 16)
                sO.inc(s.dma_start(out=dbg["dbg_pw"][:, 640:768],
                                   in_=LG[0:128, :]), 16)
                sO.inc(s.dma_start(out=dbg["dbg_pw"][:, 768:896],
                                   in_=LG[128:256, :]), 16)
                sO.inc(s.dma_start(out=dbg["dbg_pw"][:, 896:1024],
                                   in_=LG[896:1024, :]), 16)

        if stage == 104:
            arf.pop()
            arh.pop()
            arf.pop()
            return sV.n
        arf.pop()
        arh.pop()
        arf.push()
        a4 = [arf.take(4) for _ in range(8)]
        gch = [arf.take(128) for _ in range(8)]
        stg4 = arf.take(4096)
        t_marks = []
        v_marks = []
        ev_marks = {}

        for ch in range(NCH):
            par = ch % 8

            @block.vector
            def _(v, ch=ch, par=par):
                v = VEng(v)
                if ch >= 8:
                    sT.wait(v, t_marks[ch - 8])
                v.memset(a4[par][:, :], 0.0)
                v.tensor_copy(out=a4[par][:, ch % 4:ch % 4 + 1],
                              in_=amat[:, ch:ch + 1])
                v.tensor_scalar(gch[par][:, :], IOTJ[:, :],
                                rank[:, ch:ch + 1], None, ALU.is_equal)
                sV.inc(v.engine_nop())
                v_marks.append(sV.n)

            @block.tensor
            def _(t, ch=ch, par=par):
                t = TEng(t)
                sV.wait(t, v_marks[ch])
                g = ch // 4
                if ch in (64, 96):
                    sV.wait(t, ev_marks[ch // 32 - 2])
                ps = (PSA if (g // 8) % 2 == 0 else PSB)[0:4, :]
                t.matmul(ps[:, 128 * (g % 8):128 * (g % 8) + 128],
                         a4[par][:, :], gch[par][:, :],
                         start=(ch % 4 == 0), stop=(ch % 4 == 3))
                sT.inc(t.nop())
                t_marks.append(sT.n)

            if ch % 32 == 31:
                @block.vector
                def _(v, ch=ch):
                    v = VEng(v)
                    sT.wait(v)
                    b = ch // 32
                    ps = (PSA if (b % 2) == 0 else PSB)[0:4, :]
                    sV.inc(v.tensor_copy(
                        out=stg4[0:4, 1024 * b:1024 * (b + 1)], in_=ps))
                    ev_marks[ch // 32] = sV.n

        @block.gpsimd
        def _(g):
            sV.wait(g)
            sG.inc(g.dma_start(
                out=bass.AP(stag_d.tensor, 0,
                            [[128, 4], [4096, 4], [512, 8], [1, 128]]),
                in_=stg4[0:4, :].rearrange("p (b g j) -> p b g j", g=8, j=128)),
                16)

        arf.pop()
        arf.push()
        shf = [arf.take(256), arf.take(256)]
        prp = arf.take(4)
        qlo = arf.take(32)
        qhi = arf.take(32)

        @block.gpsimd
        def _(g):
            sG.wait(g)
            g.memset(shf[0][:, :], 0.0)
            sG.inc(g.dma_start(out=shf[0][:, 0:128], in_=stag_d[:, :]), 16)
            sG.inc(g.dma_start(out=prp[:, 0:1], in_=prs[0:1, 0:128]), 16)

        @block.vector
        def _(v):
            v = VEng(v)
            sG.wait(v)
            v.memset(prp[:, 2:3], 0.0)
            for qb in (64, 32, 16, 8, 4, 2, 1):
                v.tensor_single_scalar(prp[:, 1:2], prp[:, 0:1],
                                       float(128 * qb), ALU.is_ge)
                v.scalar_tensor_tensor(prp[:, 0:1], prp[:, 1:2],
                                       float(-128 * qb), prp[:, 0:1],
                                       ALU.mult, ALU.add)
                v.scalar_tensor_tensor(prp[:, 2:3], prp[:, 1:2], float(qb),
                                       prp[:, 2:3], ALU.mult, ALU.add)
            v.tensor_scalar(qlo[:, :], IOTK[:, :], prp[:, 2:3], None,
                            ALU.is_equal)
            v.tensor_single_scalar(prp[:, 2:3], prp[:, 2:3], 1.0, ALU.add)
            v.tensor_scalar(qhi[:, :], IOTK[:, :], prp[:, 2:3], None,
                            ALU.is_equal)
            cur = 0
            for bit in (64, 32, 16, 8, 4, 2, 1):
                a, b = shf[cur], shf[1 - cur]
                v.tensor_single_scalar(prp[:, 1:2], prp[:, 0:1], float(bit),
                                       ALU.is_ge)
                v.scalar_tensor_tensor(prp[:, 0:1], prp[:, 1:2], float(-bit),
                                       prp[:, 0:1], ALU.mult, ALU.add)
                v.tensor_scalar(prp[:, 3:4], prp[:, 1:2], -1.0, 1.0,
                                ALU.mult, ALU.add)
                v.tensor_scalar(b[:, :], a[:, :], prp[:, 3:4], None, ALU.mult)
                v.scalar_tensor_tensor(b[:, bit:256], a[:, 0:256 - bit],
                                       prp[:, 1:2], b[:, bit:256],
                                       ALU.mult, ALU.add)
                cur = 1 - cur
            sV.inc(v.engine_nop())

        @block.tensor
        def _(t):
            t = TEng(t)
            sV.wait(t)
            t.matmul(PSE[0:128, 0:32], shf[1][:, 0:128], qlo[:, :],
                     start=True, stop=False)
            sT.inc(t.matmul(PSE[0:128, 0:32], shf[1][:, 128:256], qhi[:, :],
                            start=False, stop=True))

        @block.vector
        def _(v):
            v = VEng(v)
            sT.wait(v)
            sV.inc(v.tensor_copy(out=IDXT[:, :], in_=PSE[0:128, 0:32]))

        if stage == 1:
            @block.scalar
            def _(s):
                sV.wait(s)
                sO.inc(s.dma_start(out=dbg["dbg_slot"][:, :], in_=IDXT[:, :]),
                       16)


        arf.pop()
        arf.pop()
        return sV.n

    def moe_compute():
        arh.push()
        W2T = [arh.take(D) for _ in range(NF)]
        bufr = [arh.take(D), arh.take(D)]
        bufq = [[arh.take(SB) for _ in range(ND)] for _ in range(2)]
        h_all = arh.take(SB * NF)
        outbf = [arh.take(D), arh.take(D)]

        @block.sync
        def _(sy):
            sV.wait(sy)
            for d in range(ND):
                sS.inc(sy.dma_start(out=W1V[d],
                                    in_=w1b_d[128 * d:128 * (d + 1), :]), 16)
            for f in range(NF):
                sS.inc(sy.dma_start(out=W2T[f],
                                    in_=w2b_d[128 * f:128 * (f + 1), :]), 16)

        gmarks = []

        @block.gpsimd
        def _(g):
            sV.wait(g)
            sC.wait(g)
            for sb in range(CAP // 128):
                if sb >= 2:
                    sG.wait(g, gmarks[sb - 2])
                sG.inc(g.indirect_dma_start(
                    bufr[sb % 2][:, :], None, LNX[:, :],
                    IndirectOffsetOnAxis(IDXT[:, sb:sb + 1], 0)), 16)
                sG.wait(g)
                sG.inc(g.dma_start(out=buf_d[128 * sb:128 * (sb + 1), :],
                                   in_=bufr[sb % 2][:, :]), 16)
                gmarks.append(sG.n)

        if stage in (2, 21, 22):
            @block.gpsimd
            def _(g):
                sG.wait(g)
                for r in range(0, CAP, 128):
                    sG.inc(g.dma_start(out=dbg["dbg_buf"][r:r + 128, :],
                                       in_=buf_d[r:r + 128, :]), 16)
        if stage == 21:
            arh.pop()
            return

        hb_marks = {}
        ffn2_marks = {}
        og_marks = []
        for sb in range(5 if stage == 23 else NSB):
            @block.scalar
            def _(s, sb=sb):
                if sb == 0:
                    sG.wait(s, gmarks[1])
                    sS.wait(s)
                else:
                    sG.wait(s, gmarks[2 * sb + 1])
                for d in range(ND):
                    sQ.inc(s.dma_start(
                        out=bufq[sb % 2][d],
                        in_=buf_d[SB * sb:SB * (sb + 1),
                                  128 * d:128 * (d + 1)],
                        transpose=True), 16)

            for f in range(NF):
                @block.tensor
                def _(t, sb=sb, f=f):
                    t = TEng(t)
                    if f == 0:
                        sQ.wait(t)
                    if f >= 2:
                        sA.wait(t, hb_marks[(sb, f - 2)])
                    hp = (PSC if f % 2 == 0 else PSD)[:, 0:SB]
                    for d in range(ND):
                        t.matmul(hp, W1V[d][:, 128 * f:128 * (f + 1)],
                                 bufq[sb % 2][d],
                                 start=(d == 0), stop=(d == ND - 1))
                    sT.inc(t.nop())

                @block.scalar
                def _(s, sb=sb, f=f):
                    sT.wait(s)
                    if f == 0 and sb >= 1:
                        sT.wait(s, ffn2_marks[sb - 1])
                    hp = (PSC if f % 2 == 0 else PSD)[:, 0:SB]
                    sA.inc(s.activation(out=h_all[:, SB * f:SB * (f + 1)],
                                        in_=hp,
                                        func=AF.Gelu, bias=B1P[:, f:f + 1],
                                        scale=1.0))
                    hb_marks[(sb, f)] = sA.n

            @block.tensor
            def _(t, sb=sb):
                t = TEng(t)
                sA.wait(t, hb_marks[(sb, NF - 1)])
                if sb >= 1:
                    sV.wait(t)
                for th, PS in ((0, PSA), (1, PSB)):
                    for hh in range(2):
                        for f in range(NF):
                            t.matmul(PS[:, 512 * hh:512 * (hh + 1)],
                                     h_all[:, SB * f + 128 * th:
                                           SB * f + 128 * (th + 1)],
                                     W2T[f][:, 512 * hh:512 * (hh + 1)],
                                     start=(f == 0), stop=(f == NF - 1))
                sT.inc(t.nop())
                ffn2_marks[sb] = sT.n

            @block.vector
            def _(v, sb=sb):
                v = VEng(v)
                sT.wait(v, ffn2_marks[sb])
                if sb >= 1:
                    sG.wait(v, og_marks[sb - 1])
                v.tensor_tensor(outbf[0][:, :], PSA[:, :], B2B[:, :], ALU.add)
                v.tensor_tensor(outbf[1][:, :], PSB[:, :], B2B[:, :], ALU.add)
                sV.inc(v.engine_nop())

            @block.gpsimd
            def _(g, sb=sb):
                sV.wait(g)
                sG.inc(g.dma_start(
                    out=oute_d[SB * sb:SB * sb + 128, :],
                    in_=outbf[0][:, :]), 16)
                sG.inc(g.dma_start(
                    out=oute_d[SB * sb + 128:SB * (sb + 1), :],
                    in_=outbf[1][:, :]), 16)
                og_marks.append(sG.n)

        if stage in (2, 22, 23):
            @block.gpsimd
            def _(g):
                sG.wait(g)
                for r in range(0, CAP, 128):
                    sG.inc(g.dma_start(out=dbg["dbg_oute"][r:r + 128, :],
                                       in_=oute_d[r:r + 128, :]), 16)
        if stage in (22, 23):
            arh.pop()
            return

        @block.gpsimd
        def _(g):
            sG.wait(g)
            sC.inc(g.collective_compute("AllGather", ALU.bypass,
                                        ins=[oute_d[:, :]], outs=[OUTA[:, :]],
                                        replica_groups=RG))
        arh.pop()

    def combine(src_dram, to_y):
        arh.push()
        ogr = [arh.take(D) for _ in range(4)]

        @block.sync
        def _(sy):
            sT.wait(sy)
            for k in range(NT):
                sS.inc(sy.dma_start(
                    out=RES[k], in_=src_dram[128 * k:128 * (k + 1), :]), 16)

        cmarks = []
        for k in range(NT):
            @block.gpsimd
            def _(g, k=k):
                if k == 0:
                    sC.wait(g)
                if k >= 2:
                    sV.wait(g, cmarks[k - 2])
                sG.inc(g.indirect_dma_start(
                    ogr[2 * (k % 2)][:, :], None, OUTA[:, :],
                    IndirectOffsetOnAxis(OGI[:, k:k + 1], 0)), 16)
                sG.inc(g.indirect_dma_start(
                    ogr[2 * (k % 2) + 1][:, :], None, OUTA[:, :],
                    IndirectOffsetOnAxis(OGI[:, 16 + k:17 + k], 0)), 16)

            @block.vector
            def _(v, k=k):
                v = VEng(v)
                sG.wait(v)
                if k == 0:
                    sS.wait(v)
                v.scalar_tensor_tensor(RES[k], ogr[2 * (k % 2)][:, :],
                                       WMY[:, k:k + 1], RES[k],
                                       ALU.mult, ALU.add)
                v.scalar_tensor_tensor(RES[k], ogr[2 * (k % 2) + 1][:, :],
                                       WMY[:, 16 + k:17 + k], RES[k],
                                       ALU.mult, ALU.add)
                sV.inc(v.engine_nop())
                cmarks.append(sV.n)

            if to_y:
                @block.scalar
                def _(s, k=k):
                    sV.wait(s)
                    sO.inc(s.dma_start(out=y_out[128 * k:128 * (k + 1), :],
                                       in_=RES[k]), 16)
        arh.pop()
        return sV.n

    def attn(lnx_mark):
        arh.push()
        WT = {p: [arh.take(D) for _ in range(ND)] for p in "kvr"}
        WoT = [arh.take(D) for _ in range(ND)]
        lnp = [arh.take(TC) for _ in range(ND)]
        pcol = arh.take(ND)
        xm = [arh.take(TC) for _ in range(ND)]
        vbf = [arh.take(TC) for _ in range(ND)]
        rbf = [arh.take(TC) for _ in range(ND)]
        ekb = [arh.take(TC) for _ in range(ND)]
        numb = [arh.take(TC) for _ in range(ND)]
        rwkv = [arh.take(TC) for _ in range(ND)]
        rcb = arh.take(TC)
        arf.push()
        ekf = arf.take(TC)
        scin = arf.take(TC + 1)
        saout = arf.take(TC + 1)
        ewt8 = [arf.take(TC + 1) for _ in range(ND)]
        carry = arf.take(ND)
        fw = arf.take(3 * TC + 2 * (TC + 1) + 2)
        o_eks = 0
        o_sbin = TC
        o_sbout = 2 * TC + 1
        o_den = 3 * TC + 2
        o_ewt16 = 4 * TC + 2
        o_sbc = 5 * TC + 3
        lm = {}

        @block.scalar
        def _(s):
            for p, wd in (("k", wtb_d[0]), ("v", wtb_d[1]), ("r", wtb_d[2])):
                for i in range(ND):
                    sQ.inc(s.dma_start(out=WT[p][i],
                                       in_=wd[:, 128 * i:128 * (i + 1)],
                                       transpose=True), 16)
            for i in range(ND):
                sQ.inc(s.dma_start(out=WoT[i],
                                   in_=wtb_d[3][:, 128 * i:128 * (i + 1)],
                                   transpose=True), 16)
            lm["wt"] = sQ.n

        @block.vector
        def _(v):
            v = VEng(v)
            v.memset(carry[:, :], 0.0)
            v.memset(fw[0:16, o_sbc:o_sbc + 1], 0.0)
            v.memset(fw[0:16, o_ewt16:o_ewt16 + TC + 1], 0.0)
            for ci in range(ND):
                v.memset(ewt8[ci][:, :], 0.0)
                v.memset(ewt8[ci][:, 0:1], 1.0)
            v.memset(fw[0:16, o_ewt16:o_ewt16 + 1], 1.0)
            sV.inc(v.engine_nop())

        @block.scalar
        def _(s):
            sV.wait(s)
            s.activation(out=fw[0:16, o_ewt16 + 1:o_ewt16 + 1 + TC],
                         in_=fw[0:16, o_ewt16 + 1:o_ewt16 + 1 + TC],
                         func=AF.Identity, bias=EWU[:, 2:3], scale=0.0)
            for ci in range(ND):
                s.activation(out=ewt8[ci][:, 1:TC + 1],
                             in_=ewt8[ci][:, 1:TC + 1],
                             func=AF.Identity, bias=EWP[:, ci:ci + 1],
                             scale=0.0)
            sA.inc(s.nop())
            lm["ewt"] = sA.n

        for tc in range(NTC):
            @block.vector
            def _(v, tc=tc):
                v = VEng(v)
                if tc > 0:
                    sQ.wait(v)
                    for ci in range(ND):
                        v.tensor_copy(out=pcol[:, ci:ci + 1],
                                      in_=lnp[ci][:, TC - 1:TC])
                sV.inc(v.engine_nop())
                lm[("pc", tc)] = sV.n

            @block.scalar
            def _(s, tc=tc):
                sV.wait(s, lm[("pc", tc)])
                if tc == 0:
                    sQ.wait(s, lnx_mark)
                else:
                    sT.wait(s, lm[("pmm", tc - 1, "r", ND - 1)])
                for ci in range(ND):
                    sQ.inc(s.dma_start(
                        out=lnp[ci],
                        in_=lnx_d[TC * tc:TC * (tc + 1),
                                  128 * ci:128 * (ci + 1)],
                        transpose=True), 16)
                lm[("lnp", tc)] = sQ.n

            for ip, p in enumerate("kvr"):
                @block.vector
                def _(v, tc=tc, ip=ip, p=p):
                    v = VEng(v)
                    sQ.wait(v, lm[("lnp", tc)])
                    if ip > 0:
                        prev = "kv"[ip - 1]
                        sT.wait(v, lm[("pmm", tc, prev, ND - 1)])
                    mo = 16 * ip
                    for ci in range(ND):
                        v.tensor_scalar(xm[ci][:, :], lnp[ci][:, :],
                                        MIX[:, mo + ci:mo + ci + 1], None,
                                        ALU.mult)
                        v.scalar_tensor_tensor(
                            xm[ci][:, 1:TC], lnp[ci][:, 0:TC - 1],
                            MIX[:, mo + 8 + ci:mo + 9 + ci], xm[ci][:, 1:TC],
                            ALU.mult, ALU.add)
                        if tc > 0:
                            v.scalar_tensor_tensor(
                                xm[ci][:, 0:1], pcol[:, ci:ci + 1],
                                MIX[:, mo + 8 + ci:mo + 9 + ci],
                                xm[ci][:, 0:1], ALU.mult, ALU.add)
                    sV.inc(v.engine_nop())
                    lm[("xm", tc, p)] = sV.n

                for o in range(ND):
                    @block.tensor
                    def _(t, tc=tc, p=p, o=o):
                        t = TEng(t)
                        if o == 0:
                            sV.wait(t, lm[("xm", tc, p)])
                            if tc == 0 and p == "k":
                                sQ.wait(t, lm["wt"])
                        if o >= 2:
                            sA.wait(t, lm[("pev", tc, p, o - 2)])
                        ps = (PSC if o % 2 == 0 else PSD)[:, 0:TC]
                        for i in range(ND):
                            t.matmul(ps, WT[p][i][:, 128 * o:128 * (o + 1)],
                                     xm[i], start=(i == 0),
                                     stop=(i == ND - 1))
                        sT.inc(t.nop())
                        lm[("pmm", tc, p, o)] = sT.n

                    @block.scalar
                    def _(s, tc=tc, p=p, o=o):
                        sT.wait(s, lm[("pmm", tc, p, o)])
                        ps = (PSC if o % 2 == 0 else PSD)[:, 0:TC]
                        if p == "k":
                            ins = s.activation(out=ekb[o][:, :], in_=ps,
                                               func=AF.Exp)
                        elif p == "v":
                            ins = s.copy(out=vbf[o][:, :], in_=ps)
                        else:
                            ins = s.activation(out=rbf[o][:, :], in_=ps,
                                               func=AF.Sigmoid)
                        sA.inc(ins)
                        lm[("pev", tc, p, o)] = sA.n

            for ci in range(ND):
                @block.vector
                def _(v, tc=tc, ci=ci):
                    v = VEng(v)
                    sA.wait(v, lm[("pev", tc, "v", ci)])
                    if tc == 0 and ci == 0:
                        sA.wait(v, lm["ewt"])
                    v.tensor_copy(out=ekf[:, :], in_=ekb[ci][:, :])
                    v.tensor_copy(out=scin[:, 0:1], in_=carry[:, ci:ci + 1])
                    v.tensor_tensor(scin[:, 1:TC + 1], ekf[:, :],
                                    vbf[ci][:, :], ALU.mult)
                    v.tensor_tensor_scan(saout[:, :], ewt8[ci][:, :],
                                         scin[:, :], 0.0, ALU.mult, ALU.add)
                    v.tensor_copy(out=carry[:, ci:ci + 1],
                                  in_=saout[:, TC:TC + 1])
                    v.scalar_tensor_tensor(numb[ci][:, :], scin[:, 1:TC + 1],
                                           EWP[:, 8 + ci:9 + ci],
                                           saout[:, 0:TC], ALU.mult, ALU.add)
                    sV.inc(v.engine_nop())
                    lm[("sa", tc, ci)] = sV.n

            @block.tensor
            def _(t, tc=tc):
                t = TEng(t)
                for ci in range(ND):
                    sA.wait(t, lm[("pev", tc, "k", ci)])
                if tc > 0:
                    sV.wait(t, lm[("rw", tc - 1, ND - 1)])
                for ci in range(ND):
                    t.matmul(PSE[0:16, 0:TC], HMM[:, 16 * ci:16 * (ci + 1)],
                             ekb[ci][:, :], start=(ci == 0),
                             stop=(ci == ND - 1))
                sT.inc(t.nop())
                lm[("eks", tc)] = sT.n

            @block.vector
            def _(v, tc=tc):
                v = VEng(v)
                sT.wait(v, lm[("eks", tc)])
                v.tensor_copy(out=fw[0:16, o_eks:o_eks + TC],
                              in_=PSE[0:16, 0:TC])
                v.tensor_copy(out=fw[0:16, o_sbin:o_sbin + 1],
                              in_=fw[0:16, o_sbc:o_sbc + 1])
                v.tensor_copy(out=fw[0:16, o_sbin + 1:o_sbin + 1 + TC],
                              in_=fw[0:16, o_eks:o_eks + TC])
                v.tensor_tensor_scan(fw[0:16, o_sbout:o_sbout + TC + 1],
                                     fw[0:16, o_ewt16:o_ewt16 + TC + 1],
                                     fw[0:16, o_sbin:o_sbin + TC + 1],
                                     0.0, ALU.mult, ALU.add)
                v.tensor_copy(out=fw[0:16, o_sbc:o_sbc + 1],
                              in_=fw[0:16, o_sbout + TC:o_sbout + TC + 1])
                v.tensor_scalar(fw[0:16, o_den:o_den + TC],
                                fw[0:16, o_eks:o_eks + TC],
                                EWU[:, 3:4], None, ALU.mult)
                v.tensor_scalar(fw[0:16, o_den:o_den + TC],
                                fw[0:16, o_den:o_den + TC],
                                CST[0:16, 2:3], None, ALU.max)
                v.tensor_tensor(fw[0:16, o_den:o_den + TC],
                                fw[0:16, o_den:o_den + TC],
                                fw[0:16, o_sbout:o_sbout + TC], ALU.add)
                v.reciprocal(out=fw[0:16, o_den:o_den + TC],
                             in_=fw[0:16, o_den:o_den + TC])
                v.tensor_copy(out=rcb[0:16, :],
                              in_=fw[0:16, o_den:o_den + TC])
                sV.inc(v.engine_nop())
                lm[("den", tc)] = sV.n

            for ci in range(ND):
                @block.tensor
                def _(t, tc=tc, ci=ci):
                    t = TEng(t)
                    sV.wait(t, lm[("den", tc)])
                    if ci >= 2:
                        sV.wait(t, lm[("rw", tc, ci - 2)])
                    ps = (PSE if ci % 2 == 0 else PSF)[0:128, 0:TC]
                    t.matmul(ps, HMT[:, 128 * ci:128 * (ci + 1)],
                             rcb[0:16, :])
                    sT.inc(t.nop())
                    lm[("rcb", tc, ci)] = sT.n

                @block.vector
                def _(v, tc=tc, ci=ci):
                    v = VEng(v)
                    sT.wait(v, lm[("rcb", tc, ci)])
                    sV.wait(v, lm[("sa", tc, ND - 1)])
                    sA.wait(v, lm[("pev", tc, "r", ci)])
                    ps = (PSE if ci % 2 == 0 else PSF)[0:128, 0:TC]
                    v.tensor_tensor(ekf[:, :], numb[ci][:, :], ps, ALU.mult)
                    v.tensor_tensor(rwkv[ci][:, :], ekf[:, :],
                                    rbf[ci][:, :], ALU.mult)
                    sV.inc(v.engine_nop())
                    lm[("rw", tc, ci)] = sV.n

            for ts in range(TC // 128):
                @block.tensor
                def _(t, tc=tc, ts=ts):
                    t = TEng(t)
                    sV.wait(t, lm[("rw", tc, ND - 1)])
                    if ts >= 2:
                        sV.wait(t, lm[("wo", tc, ts - 2)])
                    psf = (PSA if ts % 2 == 0 else PSB)
                    for hh in range(2):
                        for ci in range(ND):
                            t.matmul(psf[:, 512 * hh:512 * (hh + 1)],
                                     rwkv[ci][:, 128 * ts:128 * (ts + 1)],
                                     WoT[ci][:, 512 * hh:512 * (hh + 1)],
                                     start=(ci == 0), stop=(ci == ND - 1))
                    sT.inc(t.nop())
                    lm[("womm", tc, ts)] = sT.n

                @block.vector
                def _(v, tc=tc, ts=ts):
                    v = VEng(v)
                    sT.wait(v, lm[("womm", tc, ts)])
                    k = (TC // 128) * tc + ts
                    ps = (PSA if ts % 2 == 0 else PSB)[:, :]
                    sV.inc(v.tensor_tensor(RES[k], RES[k], ps, ALU.add))
                    lm[("wo", tc, ts)] = sV.n

        arf.pop()
        arh.pop()
        return sV.n

    def spill_res():
        @block.sync
        def _(sy):
            sV.wait(sy)
            for k in range(NT):
                sS.inc(sy.dma_start(out=x2_d[128 * k:128 * (k + 1), :],
                                    in_=RES[k]), 16)
        return sS.n

    def write_y_from_res():
        @block.scalar
        def _(s):
            sV.wait(s)
            for k in range(NT):
                sO.inc(s.dma_start(out=y_out[128 * k:128 * (k + 1), :],
                                   in_=RES[k]), 16)

    def final_wait():
        @block.scalar
        def _(s):
            sO.wait(s)
            sC.wait(s)
            sG.wait(s)
            sS.wait(s)
            sQ.wait(s)

    prologue()
    if stage == 100:
        write_y_from_res()
        final_wait()
        print("arena peaks: arf", arf.peak, "arh", arh.peak)
        ctx.close()
        return nc
    m_lnx = ln_phase()
    if stage == 101:
        write_y_from_res()
        final_wait()
        print("arena peaks: arf", arf.peak, "arh", arh.peak)
        ctx.close()
        return nc
    router_ag(m_lnx)
    if stage in (102, 103):
        write_y_from_res()
        final_wait()
        print("arena peaks: arf", arf.peak, "arh", arh.peak)
        ctx.close()
        return nc
    routing()
    if stage in (1, 104, 105):
        write_y_from_res()
        final_wait()
    elif stage in (21, 22, 23):
        moe_compute()
        write_y_from_res()
        final_wait()
    elif True:
        moe_compute()
        combine(x_in, to_y=(stage == 2))
        if stage == 2:
            final_wait()
        else:
            m_lnx2 = ln_phase(wait_v_mark=sV.n)
            attn(m_lnx2)
            if stage == 3:
                write_y_from_res()
                final_wait()
            else:
                m_lnx3 = ln_phase(wait_v_mark=sV.n)
                router_ag(m_lnx3)
                spill_res()
                routing()
                moe_compute()
                combine(x2_d, to_y=True)
                final_wait()

    print("arena peaks: arf", arf.peak, "arh", arh.peak)
    ctx.close()
    return nc


def kernel(L0_attn_x, L0_attn_mix_k, L0_attn_mix_v, L0_attn_mix_r,
           L0_attn_Wk, L0_attn_Wv, L0_attn_Wr, L0_attn_Wo,
           L0_attn_time_decay, L0_attn_time_first,
           L0_ffn_w_router, L0_ffn_b_router, L0_ffn_w1, L0_ffn_b1,
           L0_ffn_w2, L0_ffn_b2, _stage=4):
    nc = build_nc(stage=_stage)
    x = np.ascontiguousarray(np.asarray(L0_attn_x, dtype=np.float32))
    w1 = np.ascontiguousarray(np.asarray(L0_ffn_w1, dtype=np.float32))
    b1 = np.ascontiguousarray(np.asarray(L0_ffn_b1, dtype=np.float32))
    w2 = np.ascontiguousarray(np.asarray(L0_ffn_w2, dtype=np.float32))
    b2 = np.ascontiguousarray(np.asarray(L0_ffn_b2, dtype=np.float32))
    reps = {
        "Wk": np.ascontiguousarray(L0_attn_Wk, dtype=np.float32),
        "Wv": np.ascontiguousarray(L0_attn_Wv, dtype=np.float32),
        "Wr": np.ascontiguousarray(L0_attn_Wr, dtype=np.float32),
        "Wo": np.ascontiguousarray(L0_attn_Wo, dtype=np.float32),
        "mixk": np.ascontiguousarray(
            np.asarray(L0_attn_mix_k, dtype=np.float32).reshape(D)),
        "mixv": np.ascontiguousarray(
            np.asarray(L0_attn_mix_v, dtype=np.float32).reshape(D)),
        "mixr": np.ascontiguousarray(
            np.asarray(L0_attn_mix_r, dtype=np.float32).reshape(D)),
        "td": np.ascontiguousarray(L0_attn_time_decay, dtype=np.float32),
        "tf": np.ascontiguousarray(L0_attn_time_first, dtype=np.float32),
        "wrout": np.ascontiguousarray(L0_ffn_w_router, dtype=np.float32),
        "brout": np.ascontiguousarray(L0_ffn_b_router, dtype=np.float32),
    }
    in_maps = []
    for c in range(NCORE):
        m = dict(reps)
        m["x"] = np.ascontiguousarray(x[c])
        m["w1e"] = np.ascontiguousarray(w1[c])
        m["b1e"] = np.ascontiguousarray(b1[c])
        m["w2e"] = np.ascontiguousarray(w2[c])
        m["b2e"] = np.ascontiguousarray(b2[c])
        es = np.zeros(E, np.float32)
        es[c] = 1.0
        m["esel"] = es
        in_maps.append(m)
    res = run_bass_kernel_spmd(nc, in_maps, core_ids=list(range(NCORE)))
    if _stage != 4:
        return res
    return np.stack([res.results[c]["y"] for c in range(NCORE)], axis=0)
